# version 25
# baseline (speedup 1.0000x reference)
import contextlib
import numpy as np
import ml_dtypes
import concourse.bass as bass
import concourse.mybir as mybir
from concourse.bass_utils import run_bass_kernel_spmd

F32 = mybir.dt.float32
BF16 = mybir.dt.bfloat16
FP8 = mybir.dt.float8e4
AF = mybir.ActivationFunctionType
ALU = mybir.AluOpType
AX = mybir.AxisListType.X

B, T, D = 4, 4096, 1024
NHP = 8
NSLOT, QS = 4, 512
OWN = NSLOT * QS
NTT = OWN // 128
ENV = [1024, 2048, 3072, 4096]
KG = [e // 128 for e in ENV]
PAIRS = [k // 2 for k in KG]
NEXP = 8
ROLE_EXT = {0: [1024, 2048, 3072, 4096], 1: [512, 1536, 2560, 3584]}
PASS_SLOTS = {0: (0, 1), 1: (2, 3)}

SB0 = 17408
OFF_XKB = SB0 + 0
OFF_KB = SB0 + 16384
OFF_VB = SB0 + 32768
OFF_STR = SB0 + 49408
OFF_ATTN = SB0 + 82176
OFF_QALL = SB0 + 114944
OFF_WPT = SB0 + 147712
OFF_T = SB0 + 164096
OFF_SP = SB0 + 168192
OFF_P = SB0 + 172288
OFF_RS = SB0 + 180480
OFF_WQ = SB0 + 184576
OFF_MASK = SB0 + 192768
OFF_RS1 = SB0 + 200960
OFF_EYE = SB0 + 205056
OFF_RWB = SB0 + 205568
OFF_GATES = SB0 + 206080
OFF_LOG = SB0 + 206592
OFF_PROBS = SB0 + 207104
OFF_G16 = SB0 + 207616
OFF_LN16 = SB0 + 207872
OFF_BPB = SB0 + 208128
OFF_ONES = SB0 + 210176
OFF_BC = SB0 + 210432
assert OFF_BC + 1024 <= 229344

STREAM_OFF = {}
for _t in range(8):
    STREAM_OFF[_t] = OFF_STR + 4096 * _t
for _t in range(8, 12):
    STREAM_OFF[_t] = OFF_XKB + 4096 * (_t - 8)
for _t in range(12, 16):
    STREAM_OFF[_t] = OFF_T + 4096 * (_t - 12)

DEBUG = False


def build_program(debug=DEBUG):
    nc = bass.Bass("TRN2", target_bir_lowering=False, debug=False)

    d_xkf = nc.dram_tensor("xkf", [D, T], F32, kind="ExternalInput").ap()
    d_xqf = nc.dram_tensor("xqf", [D, OWN], F32, kind="ExternalInput").ap()
    d_xot = nc.dram_tensor("xot", [OWN, D], F32, kind="ExternalInput").ap()
    d_wq8 = nc.dram_tensor("wq8", [NHP, D, 384], F32, kind="ExternalInput").ap()
    d_wpT = nc.dram_tensor("wpT", [D, D], F32, kind="ExternalInput").ap()
    d_bp = nc.dram_tensor("bp", [1, D], F32, kind="ExternalInput").ap()
    d_rwT = nc.dram_tensor("rwT", [D, NEXP], F32, kind="ExternalInput").ap()
    d_ew = nc.dram_tensor("ew", [NEXP, D, D], F32, kind="ExternalInput").ap()
    d_mk = nc.dram_tensor("mk", [128, 8 * QS], BF16, kind="ExternalInput").ap()
    d_eye = nc.dram_tensor("eyein", [128, 128], F32, kind="ExternalInput").ap()
    d_out = nc.dram_tensor("out", [OWN, D], F32, kind="ExternalOutput").ap()
    d_scr = nc.dram_tensor("scr", [2 * 8, QS], F32, kind="Internal").ap()
    d_kvs = nc.dram_tensor("kvs", [NHP, 128, 4128], BF16, kind="Internal").ap()
    dbg = {}
    if debug:
        for nm, shp in [("qall", [128, 16384]), ("kb", [128, 8192]),
                        ("vb", [128, 8320]), ("attn", [128, 16384]),
                        ("h1fm", [128, 16384]), ("gates", [128, 128]),
                        ("bias16", [128, 16]), ("res2", [128, 16384])]:
            dbg[nm] = nc.dram_tensor("dbg_" + nm, shp, F32,
                                     kind="ExternalOutput").ap()

    A = nc.alloc_sbuf_tensor_at
    xkb = [A("xkb%d" % i, [128, 4096], BF16, offset=OFF_XKB + 8192 * i).ap()
           for i in range(2)]
    kb = A("kb", [128, 8192], BF16, offset=OFF_KB).ap()
    vb = A("vb", [128, 8320], BF16, offset=OFF_VB).ap()
    str8 = A("str8", [128, 8192], F32, offset=OFF_STR).ap()
    str4a = A("str4a", [128, 4096], F32, offset=OFF_XKB).ap()
    str4b = A("str4b", [128, 4096], F32, offset=OFF_T).ap()
    stream_t = [A("stream%d" % t, [128, 1024], F32, offset=STREAM_OFF[t]).ap()
                for t in range(16)]
    attn = A("attn", [128, 16384], BF16, offset=OFF_ATTN).ap()
    h1fm = A("h1fm", [128, 16384], BF16, offset=OFF_ATTN).ap()
    qall = A("qall", [128, 16384], BF16, offset=OFF_QALL).ap()
    wpT = A("wpT", [128, 8192], BF16, offset=OFF_WPT).ap()
    ewh2 = A("ewh2", [128, 8192], BF16, offset=OFF_WPT).ap()
    xq0 = A("xq0", [128, 8192], BF16, offset=OFF_T).ap()
    xq1 = A("xq1", [128, 8192], BF16, offset=OFF_ATTN).ap()
    t_exp = A("t_exp", [128, 2048], BF16, offset=OFF_T).ap()
    spb = A("spb", [128, 2048], BF16, offset=OFF_SP).ap()
    pbb = A("pbb", [128, 4096], BF16, offset=OFF_P).ap()
    bcb = A("bcb", [128, 512], BF16, offset=OFF_BC).ap()
    rsb2 = [A("rsb0", [128, 1024], F32, offset=OFF_RS).ap(),
            A("rsb1", [128, 1024], F32, offset=OFF_RS1).ap()]
    wqK = [A("wqK%d" % i, [128, 2048], BF16, offset=OFF_WQ + 4096 * i).ap()
           for i in range(2)]
    wqA = [A("wqA%d" % i, [128, 3072], BF16, offset=OFF_STR + 6144 * i).ap()
           for i in range(2)]
    mkb = A("mkb", [128, 4096], BF16, offset=OFF_MASK).ap()
    eyeb = A("eyeb", [128, 128], F32, offset=OFF_EYE).ap()
    rwb = A("rwb", [128, 64], BF16, offset=OFF_RWB).ap()
    gates = A("gates", [128, 128], F32, offset=OFF_GATES).ap()
    logit = A("logit", [128, 128], F32, offset=OFF_LOG).ap()
    probs = A("probs", [128, 128], F32, offset=OFF_PROBS).ap()
    g_v1 = A("g_v1", [128, 16], F32, offset=OFF_G16).ap()
    g_v2 = A("g_v2", [128, 16], F32, offset=OFF_G16 + 64).ap()
    g_rd = A("g_rd", [128, 16], F32, offset=OFF_G16 + 128).ap()
    g_t = A("g_t", [128, 16], F32, offset=OFF_G16 + 192).ap()
    lnm16 = A("lnm16", [128, 16], F32, offset=OFF_LN16).ap()
    lnq16 = A("lnq16", [128, 16], F32, offset=OFF_LN16 + 64).ap()
    lnv16 = A("lnv16", [128, 16], F32, offset=OFF_LN16 + 128).ap()
    lnr16 = A("lnr16", [128, 16], F32, offset=OFF_LN16 + 192).ap()
    bpb = A("bpb", [1, 1024], BF16, offset=OFF_BPB).ap()
    onesr = A("onesr", [1, 128], BF16, offset=OFF_ONES).ap()

    def ap(base, off, dims):
        return bass.AP(base.tensor, off, dims)

    recs = {k: [] for k in "tsvgy"}
    SEMN = ("T", "S", "V", "LO", "XS", "WQ0", "WQ1", "XB0", "XB1",
            "KS0", "KS1", "KL0", "KL1", "SCR0", "SCR1", "BC", "EW0", "EW1",
            "WPT", "SET", "LSC", "EYE", "LSTR", "DBG")
    ctr = {k: 0 for k in SEMN}

    with contextlib.ExitStack() as ctx:
        En = ctx.enter_context
        psS = En(nc.psum_tensor("psS", [128, 2048], F32)).ap()
        psA = En(nc.psum_tensor("psA", [128, 512], F32)).ap()
        psB = En(nc.psum_tensor("psB", [128, 512], F32)).ap()
        psK = En(nc.psum_tensor("psK", [128, 512], F32)).ap()
        psV = En(nc.psum_tensor("psV", [128, 512], F32)).ap()
        sem = {n: En(nc.semaphore("s" + n)) for n in SEMN}

        def emit(e, method, *args, w=(), i=(), **kw):
            out = []
            for nm, n in i:
                ctr[nm] += n
                out.append(ctr[nm])
            recs[e].append((tuple(w), method, args, kw, tuple(i)))
            return out[0] if len(out) == 1 else (out or 0)

        ld_mask = emit("y", "dma_start", mkb, d_mk, i=[("LSC", 16)])
        ld_eye = emit("y", "dma_start", eyeb, d_eye, i=[("EYE", 16)])
        ld_stream1 = 0
        ld_xqp = []

        def load_xq_parts():
            for part in range(4):
                xdst = xq0 if part < 2 else xq1
                ld_xqp.append(emit(
                    "g", "dma_start",
                    ap(xdst, 4096 * (part % 2),
                       [[8192, 128], [2048, 2], [1, 2048]]),
                    ap(d_xqf, 256 * OWN * part,
                       [[2048, 128], [2048 * 128, 2], [1, 2048]]),
                    i=[("XS", 16)]))
            emit("g", "dma_start", bpb, d_bp, i=[("SET", 16)])
            return emit("g", "dma_start", rwb,
                        ap(d_rwT, 0, [[8, 128], [8 * 128, 8], [1, 8]]),
                        i=[("SET", 16)])
        ld_wpT = 4 * 16

        def load_wpT_chunk(c):
            emit("g", "dma_start",
                 ap(wpT, 2048 * c, [[8192, 128], [1024, 2], [1, 1024]]),
                 ap(d_wpT, 131072 * 2 * c, [[1024, 128], [131072, 2], [1, 1024]]),
                 i=[("WPT", 16)])
        emit("v", "memset",
             ap(vb, 64, [[8320, 128], [4160, 2], [130, 32], [65, 2]]), 1.0,
             i=[("V", 1)])
        emit("v", "memset", onesr, 1.0, i=[("V", 1)])
        emit("v", "memset", rsb2[0], 1.0, i=[("V", 1)])
        v_ones = emit("v", "memset", rsb2[1], 1.0, i=[("V", 1)])

        ld_wq = {}

        def load_wq_usage(k, guard_t):
            hp = k % 8
            qn = "WQ%d" % (k % 2)
            wts = [(sem["T"], guard_t)] if guard_t else []
            if k < 8:
                dst = wqA[k % 2]
                src = ap(d_wq8, hp * D * 384,
                         [[384, 128], [384 * 128, 8], [1, 384]])
            else:
                dst = wqK[k % 2]
                src = ap(d_wq8, hp * D * 384 + 128,
                         [[384, 128], [384 * 128, 8], [1, 256]])
            ld_wq[k] = (qn, emit("g", "dma_start", dst, src,
                                 w=wts, i=[(qn, 16)]))

        load_wq_usage(0, 0)
        load_wq_usage(1, 0)
        ld_small = load_xq_parts()

        xk_load = {}
        xkb_guard = {0: 0, 1: 0}

        def load_xk(p, hp, tc):
            bi = tc % 2
            qn = "XB%d" % bi
            wts = [(sem["T"], xkb_guard[bi])] if xkb_guard[bi] else []
            xk_load[(p, hp, tc)] = (qn, emit(
                "g", "dma_start", xkb[bi],
                ap(d_xkf, 512 * tc, [[4096, 128], [4096 * 128, 8], [1, 512]]),
                w=wts, i=[(qn, 16)]))

        load_xk(0, 0, 0)
        load_xk(0, 0, 1)

        evac_rot = {r: (sem["S"], 0) for r in range(4)}
        qall_s = {}
        guard = {}
        for hp in range(NHP):
            for qt in range(NSLOT):
                gi = hp * NSLOT + qt
                rot = gi % 4
                ps = psS[:, 512 * rot:512 * rot + 512]
                wsem, wval = evac_rot[rot]
                qn, qv = ld_wq[hp]
                wts = [(sem[qn], qv), (sem["XS"], ld_xqp[0])]
                if wval:
                    wts.append((wsem, wval))
                lq = 0
                for c in range(8):
                    xsrc = xq0 if c < 4 else xq1
                    cc = c if c < 4 else c - 4
                    if c == 0:
                        wc = wts
                    elif c in (2, 4, 6):
                        wc = [(sem["XS"], ld_xqp[c // 2])]
                    else:
                        wc = ()
                    lq = emit(
                        "t", "matmul", ps,
                        wqA[hp % 2][:, 384 * c:384 * c + 128],
                        xsrc[:, 2048 * cc + 512 * qt:2048 * cc + 512 * qt + 512],
                        start=(c == 0), stop=(c == 7),
                        w=wc,
                        i=([("T", 1)] if c == 7 else ()))
                col = 2048 * hp + 512 * qt
                sv = emit("s", "copy", qall[:, col:col + 512],
                          ps, w=[(sem["T"], lq)], i=[("S", 1)])
                evac_rot[rot] = (sem["S"], sv)
                qall_s[hp] = sv
                guard[hp] = lq
            if hp + 2 < NHP:
                load_wq_usage(hp + 2, guard[hp])
        load_wq_usage(8, guard[6])
        load_wq_usage(9, guard[7])

        bg_rot = {0: (sem["S"], 0), 1: (sem["S"], 0)}
        bg_i = [0]

        def bg_bank():
            r = bg_i[0] % 2
            bg_i[0] += 1
            return r, (psK if r == 0 else psV)

        kv_tlast = {}
        kv_kevac = {}
        kv_vevac = {}
        kv_reload = {}
        ks_val = {0: 0, 1: 0}
        t_av_box = [0]

        def kv_groups(hp, p, si):
            buf = hp % 2
            kbase, vbase = 4096 * buf, 4160 * buf
            use = 8 + si
            tcs = (0, 1, 2, 3) if p == 0 else (4, 5, 6, 7)
            kv_kevac[(hp, p)] = {}
            kv_vevac[(hp, p)] = {}
            if p == 1:
                kn, sn = "KL%d" % buf, "KS%d" % buf
                w_r = [(sem["T"], t_av_box[0]), (sem[sn], ks_val[buf])]
                emit("y", "dma_start", kb[:, kbase:kbase + 2048],
                     ap(d_kvs, hp * 128 * 4128, [[4128, 128], [1, 2048]]),
                     w=w_r, i=[(kn, 16)])
                kl = emit("y", "dma_start", vb[:, vbase:vbase + 2080],
                          ap(d_kvs, hp * 128 * 4128 + 2048,
                             [[4128, 128], [1, 2080]]),
                          w=w_r, i=[(kn, 16)])
                kv_reload[(hp, p)] = (kn, kl)
            for tc in tcs[:2]:
                if (p, hp, tc) not in xk_load:
                    load_xk(p, hp, tc)
            for tc in tcs:
                bi = tc % 2
                qx, vx = xk_load[(p, hp, tc)]
                qn, qv = ld_wq[use]

                def gK(tc=tc, bi=bi, qx=qx, vx=vx, qn=qn, qv=qv,
                       kbase=kbase, buf=buf, p=p, hp=hp):
                    r, psx = bg_bank()
                    wsem, wval = bg_rot[r]
                    wts = [(sem[qn], qv), (sem[qx], vx)]
                    if wval:
                        wts.append((wsem, wval))
                    tm = 0
                    for c in range(8):
                        tm = emit("t", "matmul", psx[:, 0:512],
                                  wqK[buf][:, 256 * c:256 * c + 128],
                                  xkb[bi][:, 512 * c:512 * c + 512],
                                  start=(c == 0), stop=(c == 7),
                                  w=(wts if c == 0 else ()),
                                  i=([("T", 1)] if c == 7 else ()))
                    wev = [(sem["T"], tm)]
                    if p == 0 and ks_val[buf]:
                        wev.append((sem["KS%d" % buf], ks_val[buf]))
                    vv = emit("v", "tensor_copy",
                              kb[:, kbase + 512 * tc:kbase + 512 * tc + 512],
                              psx[:, 0:512], w=wev, i=[("V", 1)])
                    bg_rot[r] = (sem["V"], vv)
                    kv_tlast[(hp, p)] = tm
                    kv_kevac[(hp, p)][tc] = vv
                yield gK
                for i4 in range(4):
                    def gV(tc=tc, bi=bi, i4=i4, qx=qx, vx=vx, qn=qn, qv=qv,
                           kbase=kbase, vbase=vbase, buf=buf, p=p, hp=hp,
                           tcs=tcs):
                        tv = 4 * tc + i4
                        r, psx = bg_bank()
                        wsem, wval = bg_rot[r]
                        wts = [(sem[qn], qv), (sem[qx], vx)]
                        if wval:
                            wts.append((wsem, wval))
                        tm = 0
                        for c in range(8):
                            tm = emit(
                                "t", "matmul", psx[:, 0:128],
                                xkb[bi][:, 512 * c + 128 * i4:512 * c + 128 * i4 + 128],
                                wqK[buf][:, 256 * c + 128:256 * c + 256],
                                start=(c == 0), stop=(c == 7),
                                w=(wts if c == 0 else ()),
                                i=([("T", 1)] if c == 7 else ()))
                        wev = [(sem["T"], tm)]
                        if p == 0 and ks_val[buf]:
                            wev.append((sem["KS%d" % buf], ks_val[buf]))
                        vv = emit("v", "tensor_copy",
                                  ap(vb, vbase + 130 * tv,
                                     [[8320, 128], [65, 2], [1, 64]]),
                                  ap(psK if r == 0 else psV, 0,
                                     [[512, 128], [64, 2], [1, 64]]),
                                  w=wev, i=[("V", 1)])
                        bg_rot[r] = (sem["V"], vv)
                        kv_tlast[(hp, p)] = tm
                        kv_vevac[(hp, p)][tv] = vv
                        if i4 == 3:
                            xkb_guard[bi] = tm
                            if tc + 2 in tcs:
                                load_xk(p, hp, tc + 2)
                            if p == 0 and tc == tcs[-1]:
                                sn = "KS%d" % buf
                                wsp = [(sem["V"], vv)]
                                emit("y", "dma_start",
                                     ap(d_kvs, hp * 128 * 4128,
                                        [[4128, 128], [1, 2048]]),
                                     kb[:, kbase:kbase + 2048],
                                     w=wsp, i=[(sn, 16)])
                                ks_val[buf] = emit(
                                    "y", "dma_start",
                                    ap(d_kvs, hp * 128 * 4128 + 2048,
                                       [[4128, 128], [1, 2080]]),
                                    vb[:, vbase:vbase + 2080],
                                    w=wsp, i=[(sn, 16)])
                    yield gV

        s_exp_prev = 0
        s_ln_prev = 0
        v_sq_prev = 0
        t_av_prev = ctr["T"]
        av_evac_prev = 0
        v_apply_last = 0
        lg_scr = {0: 0, 1: 0}
        rs_copy_last = 0
        av_queue = []
        epi_q = []
        work_q = []
        hp_start_v_cur = [0]
        gpair = [0]
        mask_waited = [False]
        epi_pops = [0]
        P0_EPI = NHP * 9

        def emit_av(slot, j, hp_, p_, v_sq, par):
            nonlocal t_av_prev
            nkg = KG[slot]
            vbase = 4160 * (hp_ % 2)
            pb0 = 2048 * par
            for kk in range(2):
                kgi = 2 * j + kk
                for h in range(2):
                    ps = psA if h == 0 else psB
                    wts = []
                    if kk == 0 and h == 0:
                        wts.append((sem["V"], v_sq))
                        tvn = 2 * j + 1
                        if p_ == 0 or tvn >= 16:
                            wts.append((sem["V"], kv_vevac[(hp_, p_)][tvn]))
                        if kgi == 0 and av_evac_prev:
                            wts.append((sem["V"], av_evac_prev))
                    t_av_prev = emit(
                        "t", "matmul", ps[0:65, :],
                        vb[:, vbase + 130 * kgi + 65 * h:vbase + 130 * kgi + 65 * h + 65],
                        pbb[:, pb0 + 1024 * kk + 512 * h:pb0 + 1024 * kk + 512 * h + 512],
                        start=(kgi == 0), stop=(kgi == nkg - 1),
                        w=wts, i=([("T", 1)] if (kk == 1 and h == 1) else ()))
            t_av_box[0] = t_av_prev

        def pop_av():
            nonlocal av_evac_prev, rs_copy_last
            slot_, j_, vsq_, par_, hp_, p_, last_ = av_queue.pop(0)
            emit_av(slot_, j_, hp_, p_, vsq_, par_)
            if last_:
                p_ = hp_ % 2
                rsb = rsb2[p_]
                acol = 2048 * hp_ + 512 * slot_
                emit("v", "tensor_copy", attn[0:64, acol:acol + 512],
                     psA[0:64, :], w=[(sem["T"], t_av_prev)], i=[("V", 1)])
                emit("v", "tensor_copy", attn[64:128, acol:acol + 512],
                     psB[0:64, :], i=[("V", 1)])
                kA, kB = 2 * slot_, 2 * slot_ + 1
                rs_w = [(sem["SCR%d" % p_], lg_scr[p_])] if (
                    slot_ % 2 == 0 and lg_scr[p_]) else []
                emit("v", "tensor_copy",
                     rsb[32 * (kA % 4):32 * (kA % 4) + 1,
                         512 * (kA // 4):512 * (kA // 4) + 512],
                     psA[64:65, :], w=rs_w, i=[("V", 1)])
                av_evac_prev = emit("v", "tensor_copy",
                                    rsb[32 * (kB % 4):32 * (kB % 4) + 1,
                                        512 * (kB // 4):512 * (kB // 4) + 512],
                                    psB[64:65, :], i=[("V", 1)])
                rs_copy_last = av_evac_prev
                if slot_ % 2 == 1:
                    queue_epilogue(hp_, slot_ // 2)

        class EpiState:
            vv = 0
            v_recip = 0

        def queue_epilogue(hp_, pa):
            es = EpiState()
            p_ = hp_ % 2
            rsb = rsb2[p_]
            scr_n = "SCR%d" % p_
            ch = 512 * pa

            def t_eps():
                es.vv = emit("v", "tensor_scalar", rsb[:, ch:ch + 512],
                             rsb[:, ch:ch + 512], 1e-9, None,
                             ALU.add, w=[(sem["V"], rs_copy_last)],
                             i=[("V", 1)])

            def mk_recip(k4):
                def t_recip():
                    es.v_recip = emit("v", "reciprocal",
                                      rsb[:, 256 * k4:256 * k4 + 256],
                                      rsb[:, 256 * k4:256 * k4 + 256],
                                      w=[(sem["V"], es.vv)], i=[("V", 1)])
                return t_recip

            def mk_scr(k):
                def t_scr():
                    scr_w = [(sem["V"], es.v_recip)]
                    if v_apply_last:
                        scr_w.append((sem["V"], v_apply_last))
                    lg_scr[p_] = emit(
                        "g", "dma_start",
                        ap(d_scr, (p_ * 8 + k) * QS, [[512, 1], [1, 512]]),
                        rsb[32 * (k % 4):32 * (k % 4) + 1,
                            512 * (k // 4):512 * (k // 4) + 512],
                        w=scr_w, i=[(scr_n, 16)])
                return t_scr

            def mk_bc(s2):
                def t_bc():
                    nonlocal v_apply_last
                    lg2 = 0
                    for h in range(2):
                        bc_w = [(sem[scr_n], lg_scr[p_])]
                        if v_apply_last:
                            bc_w.append((sem["V"], v_apply_last))
                        lg2 = emit("g", "dma_start",
                                   bcb[64 * h:64 * h + 64, :],
                                   ap(d_scr, (p_ * 8 + 2 * s2 + h) * QS,
                                      [[0, 64], [1, 512]]),
                                   w=(bc_w if h == 0 else ()),
                                   i=[("BC", 16)])
                    acol = 2048 * hp_ + 512 * s2
                    v_apply_last = emit(
                        "v", "tensor_tensor", attn[:, acol:acol + 512],
                        attn[:, acol:acol + 512], bcb[:, :], ALU.mult,
                        w=[(sem["BC"], lg2)], i=[("V", 1)])
                return t_bc

            epi_q.append(t_eps)
            for k4 in (2 * pa, 2 * pa + 1):
                epi_q.append(mk_recip(k4))
            for k in range(4 * pa, 4 * pa + 4):
                epi_q.append(mk_scr(k))
            for s2 in (2 * pa, 2 * pa + 1):
                epi_q.append(mk_bc(s2))

        def post_thunks(half):
            t0 = 8 * half
            th = []
            st = {}

            def ew_slice(c, hh):
                if half == 0:
                    return qall[:, 2048 * c + 512 * hh:2048 * c + 512 * hh + 512]
                return ewh2[:, 1024 * c + 512 * hh:1024 * c + 512 * hh + 512]

            def dump_dst():
                return qall[:, 0:1024] if half == 0 else qall[:, 1024:2048]

            def mk_proj(t, hh):
                def f():
                    r, psx = bg_bank()
                    wsem, wval = bg_rot[r]
                    wts = [(sem["WPT"], ld_wpT), (sem["V"], v_apply_last),
                           (sem["SET"], ld_small), (sem["V"], v_ones)]
                    if wval:
                        wts.append((wsem, wval))
                    tm = 0
                    for c in range(8):
                        tm = emit(
                            "t", "matmul", psx[:, 0:512],
                            attn[:, 2048 * c + 128 * t:2048 * c + 128 * t + 128],
                            wpT[:, 1024 * c + 512 * hh:1024 * c + 512 * hh + 512],
                            start=(c == 0), stop=False,
                            w=(wts if c == 0 else ()),
                            i=([("T", 1)] if c == 7 else ()))
                    tb = emit("t", "matmul", psx[:, 0:512],
                              onesr, bpb[:, 512 * hh:512 * hh + 512],
                              start=False, stop=True,
                              w=[(sem["T"], tm)], i=[("T", 1)])
                    vv = emit("v", "tensor_tensor",
                              stream_t[t][:, 512 * hh:512 * hh + 512],
                              psx[:, 0:512],
                              stream_t[t][:, 512 * hh:512 * hh + 512],
                              ALU.add,
                              w=[(sem["T"], tb),
                                 (sem["LSTR"], st["ld_stream"])],
                              i=[("V", 1)])
                    bg_rot[r] = (sem["V"], vv)
                    st[("res", t, hh)] = vv
                return f

            def mk_ln1_stats(t):
                def f():
                    v1 = emit("v", "reduce_sum", lnm16[:, t:t + 1],
                              stream_t[t], AX,
                              w=[(sem["V"], st[("res", t, 0)]),
                                 (sem["V"], st[("res", t, 1)])],
                              i=[("V", 1)])
                    wq_ = [(sem["V"], v1)]
                    if st.get(("dumpchain",), 0):
                        wq_.append((sem["V"], st[("dumpchain",)]))
                    vq = emit("v", "scalar_tensor_tensor", dump_dst(),
                              stream_t[t], 1.0, stream_t[t],
                              ALU.mult, ALU.mult,
                              w=wq_, i=[("V", 1)],
                              accum_out=lnq16[:, t:t + 1])
                    st[("stats", t)] = vq
                    st[("dumpchain",)] = vq
                return f

            def mk_ln_batch(which, c0, cnt=8):
                def f():
                    key = "stats" if which.startswith("ln1") else "stats2"
                    wts = [(sem["V"], st[(key, t)])
                           for t in range(c0, c0 + cnt)]
                    v2 = emit("v", "tensor_scalar", lnm16[:, c0:c0 + cnt],
                              lnm16[:, c0:c0 + cnt], 1.0 / D, None, ALU.mult,
                              w=wts, i=[("V", 1)])
                    v3 = emit("v", "tensor_scalar", lnq16[:, c0:c0 + cnt],
                              lnq16[:, c0:c0 + cnt], 1.0 / D, None, ALU.mult,
                              w=[(sem["V"], v2)], i=[("V", 1)])
                    v4 = emit("v", "tensor_tensor", lnv16[:, c0:c0 + cnt],
                              lnm16[:, c0:c0 + cnt], lnm16[:, c0:c0 + cnt],
                              ALU.mult, w=[(sem["V"], v3)], i=[("V", 1)])
                    v5 = emit("v", "tensor_tensor", lnv16[:, c0:c0 + cnt],
                              lnq16[:, c0:c0 + cnt], lnv16[:, c0:c0 + cnt],
                              ALU.subtract, w=[(sem["V"], v4)], i=[("V", 1)])
                    v5e = emit("v", "tensor_scalar", lnv16[:, c0:c0 + cnt],
                               lnv16[:, c0:c0 + cnt], 1e-5, None, ALU.add,
                               w=[(sem["V"], v5)], i=[("V", 1)])
                    sx = emit("s", "activation", lnr16[:, c0:c0 + cnt],
                              lnv16[:, c0:c0 + cnt], AF.Ln,
                              w=[(sem["V"], v5e)], i=[("S", 1)])
                    sy = emit("s", "activation", lnr16[:, c0:c0 + cnt],
                              lnr16[:, c0:c0 + cnt], AF.Exp, scale=-0.5,
                              w=[(sem["S"], sx)], i=[("S", 1)])
                    st[("rstd", which)] = sy
                return f

            def mk_ln1_apply(t):
                def f():
                    vv = emit("v", "tensor_scalar", stream_t[t], stream_t[t],
                              lnm16[:, t:t + 1], lnr16[:, t:t + 1],
                              ALU.subtract, ALU.mult,
                              w=[(sem["S"], st[("rstd", "ln1")])],
                              i=[("V", 1)])
                    st[("h1", t)] = vv
                return f

            def mk_tr(t, hh):
                def f():
                    r, psx = bg_bank()
                    wsem, wval = bg_rot[r]
                    wts = [(sem["V"], st[("h1", t)]), (sem["EYE"], ld_eye)]
                    if wval:
                        wts.append((wsem, wval))
                    tm = 0
                    for c4 in range(4):
                        c = 4 * hh + c4
                        tm = emit("t", "transpose",
                                  psx[:, 128 * c4:128 * c4 + 128],
                                  stream_t[t][:, 128 * c:128 * c + 128],
                                  eyeb,
                                  w=(wts if c4 == 0 else ()),
                                  i=([("T", 1)] if c4 == 3 else ()))
                    vv = emit("v", "tensor_copy",
                              ap(h1fm, 2048 * (4 * hh) + 128 * t,
                                 [[16384, 128], [2048, 4], [1, 128]]),
                              psx[:, 0:512],
                              w=[(sem["T"], tm)], i=[("V", 1)])
                    bg_rot[r] = (sem["V"], vv)
                    st[("h1fm", t, hh)] = vv
                return f

            def mk_router(t):
                def f():
                    r, psx = bg_bank()
                    wsem, wval = bg_rot[r]
                    wts = [(sem["V"], st[("h1fm", t, 0)]),
                           (sem["V"], st[("h1fm", t, 1)]),
                           (sem["SET"], ld_small)]
                    if wval:
                        wts.append((wsem, wval))
                    tm = 0
                    for c in range(8):
                        tm = emit("t", "matmul", psx[:, 0:8],
                                  h1fm[:, 2048 * c + 128 * t:2048 * c + 128 * t + 128],
                                  rwb[:, 8 * c:8 * c + 8],
                                  start=(c == 0), stop=(c == 7),
                                  w=(wts if c == 0 else ()),
                                  i=([("T", 1)] if c == 7 else ()))
                    vv = emit("v", "tensor_copy", logit[:, 8 * t:8 * t + 8],
                              psx[:, 0:8], w=[(sem["T"], tm)], i=[("V", 1)])
                    bg_rot[r] = (sem["V"], vv)
                    st[("logit", t)] = vv
                return f

            def gating():
                c0 = 8 * t0
                g0 = t0
                pr_ = ap(probs, c0, [[128, 128], [8, 8], [1, 8]])
                ga_ = ap(gates, c0, [[128, 128], [8, 8], [1, 8]])
                fl_p = ap(probs, c0, [[128, 128], [1, 64]])
                fl_g = ap(gates, c0, [[128, 128], [1, 64]])

                def b16(a):
                    return ap(a, g0, [[16, 128], [1, 8], [0, 8]])

                wts = [(sem["V"], st[("logit", t)]) for t in range(t0, t0 + 8)]
                sx = emit("s", "activation", fl_p,
                          ap(logit, c0, [[128, 128], [1, 64]]), AF.Exp,
                          w=wts, i=[("S", 1)])
                vv = emit("v", "reduce_sum", g_rd[:, g0:g0 + 8], pr_, AX,
                          w=[(sem["S"], sx)], i=[("V", 1)])
                vv = emit("v", "reciprocal", g_rd[:, g0:g0 + 8],
                          g_rd[:, g0:g0 + 8], w=[(sem["V"], vv)], i=[("V", 1)])
                vv = emit("v", "tensor_tensor", fl_p, fl_p, b16(g_rd),
                          ALU.mult, w=[(sem["V"], vv)], i=[("V", 1)])
                vv = emit("v", "reduce_max", g_v1[:, g0:g0 + 8], pr_, AX,
                          w=[(sem["V"], vv)], i=[("V", 1)])
                vv = emit("v", "tensor_tensor", fl_g, fl_p, b16(g_v1),
                          ALU.is_ge, w=[(sem["V"], vv)], i=[("V", 1)])
                vv = emit("v", "scalar_tensor_tensor", fl_g, fl_g, -100.0,
                          fl_p, ALU.mult, ALU.add,
                          w=[(sem["V"], vv)], i=[("V", 1)])
                vv = emit("v", "reduce_max", g_v2[:, g0:g0 + 8], ga_, AX,
                          w=[(sem["V"], vv)], i=[("V", 1)])
                vv = emit("v", "tensor_tensor", fl_g, fl_p, b16(g_v2),
                          ALU.is_ge, w=[(sem["V"], vv)], i=[("V", 1)])
                vv = emit("v", "tensor_tensor", g_rd[:, g0:g0 + 8],
                          g_v1[:, g0:g0 + 8], g_v2[:, g0:g0 + 8], ALU.add,
                          w=[(sem["V"], vv)], i=[("V", 1)])
                vv = emit("v", "tensor_scalar", g_rd[:, g0:g0 + 8],
                          g_rd[:, g0:g0 + 8], 1e-6, None, ALU.add,
                          w=[(sem["V"], vv)], i=[("V", 1)])
                vv = emit("v", "reciprocal", g_rd[:, g0:g0 + 8],
                          g_rd[:, g0:g0 + 8], w=[(sem["V"], vv)], i=[("V", 1)])
                vv = emit("v", "tensor_tensor", fl_g, fl_g, fl_p, ALU.mult,
                          w=[(sem["V"], vv)], i=[("V", 1)])
                vv = emit("v", "tensor_tensor", fl_g, fl_g, b16(g_rd),
                          ALU.mult, w=[(sem["V"], vv)], i=[("V", 1)])
                vv = emit("v", "tensor_tensor", g_t[:, g0:g0 + 8],
                          g_v1[:, g0:g0 + 8], g_v1[:, g0:g0 + 8], ALU.mult,
                          w=[(sem["V"], vv)], i=[("V", 1)])
                vv = emit("v", "tensor_tensor", g_v2[:, g0:g0 + 8],
                          g_v2[:, g0:g0 + 8], g_v2[:, g0:g0 + 8], ALU.mult,
                          w=[(sem["V"], vv)], i=[("V", 1)])
                vv = emit("v", "tensor_tensor", g_t[:, g0:g0 + 8],
                          g_t[:, g0:g0 + 8], g_v2[:, g0:g0 + 8], ALU.add,
                          w=[(sem["V"], vv)], i=[("V", 1)])
                v_g = emit("v", "tensor_tensor", g_t[:, g0:g0 + 8],
                           g_t[:, g0:g0 + 8], g_rd[:, g0:g0 + 8], ALU.mult,
                           w=[(sem["V"], vv)], i=[("V", 1)])
                st[("gates",)] = v_g

            def mk_ewload(e, hh, ci):
                def f():
                    wts = []
                    u = st.get(("ewuse", hh), 0)
                    if u:
                        wts.append((sem["T"], u))
                    if e == 0 and half == 0:
                        wts += [(sem["V"], st[("stats", t)])
                                for t in range(t0, t0 + 8)]
                    if e == 0 and half == 1:
                        wts.append((sem["T"], st[("projdone",)]))
                    if half == 0:
                        dst = ap(qall, 512 * hh + 2048 * 2 * ci,
                                 [[16384, 128], [2048, 2], [1, 512]])
                    else:
                        dst = ap(ewh2, 512 * hh + 1024 * 2 * ci,
                                 [[8192, 128], [1024, 2], [1, 512]])
                    st[("ewld", e, hh)] = emit(
                        "g", "dma_start", dst,
                        ap(d_ew, e * D * D + 512 * hh + 131072 * 2 * ci,
                           [[1024, 128], [131072, 2], [1, 512]]),
                        w=wts, i=[("EW%d" % hh, 16)])
                return f

            def mk_exp(e, t, hh):
                def f():
                    r, psx = bg_bank()
                    wsem, wval = bg_rot[r]
                    wts = [(sem["EW%d" % hh], st[("ewld", e, hh)]),
                           (sem["V"], st[("gates",)])]
                    if wval:
                        wts.append((wsem, wval))
                    tm = 0
                    for c in range(8):
                        tm = emit(
                            "t", "matmul", psx[:, 0:512],
                            h1fm[:, 2048 * c + 128 * t:2048 * c + 128 * t + 128],
                            ew_slice(c, hh),
                            start=(c == 0), stop=(c == 7),
                            w=(wts if c == 0 else ()),
                            i=([("T", 1)] if c == 7 else ()))
                    st[("ewuse", hh)] = tm
                    prev = st.get(("moe", t, hh), st[("h1", t)])
                    vv = emit("v", "scalar_tensor_tensor",
                              stream_t[t][:, 512 * hh:512 * hh + 512],
                              psx[:, 0:512],
                              gates[:, 8 * t + e:8 * t + e + 1],
                              stream_t[t][:, 512 * hh:512 * hh + 512],
                              ALU.mult, ALU.add,
                              w=[(sem["T"], tm), (sem["V"], prev)],
                              i=[("V", 1)])
                    bg_rot[r] = (sem["V"], vv)
                    st[("moe", t, hh)] = vv
                return f

            def mk_ln2_stats(t):
                def f():
                    vb_ = emit("v", "tensor_scalar", stream_t[t],
                               stream_t[t], g_t[:, t:t + 1], None, ALU.add,
                               w=[(sem["V"], st[("moe", t, 0)]),
                                  (sem["V"], st[("moe", t, 1)]),
                                  (sem["S"], st[("rstd", "ln1")])],
                               i=[("V", 1)])
                    v1 = emit("v", "reduce_sum", lnm16[:, t:t + 1],
                              stream_t[t], AX,
                              w=[(sem["V"], vb_)], i=[("V", 1)])
                    wq_ = [(sem["V"], v1),
                           (sem["V"], st[("dumpchain",)]),
                           (sem["EW1"], st[("ewld", NEXP - 1, 1)]),
                           (sem["T"], st[("ewuse", 0)]),
                           (sem["T"], st[("ewuse", 1)])]
                    vq = emit("v", "scalar_tensor_tensor", dump_dst(),
                              stream_t[t], 1.0, stream_t[t],
                              ALU.mult, ALU.mult,
                              w=wq_, i=[("V", 1)],
                              accum_out=lnq16[:, t:t + 1])
                    st[("stats2", t)] = vq
                    st[("dumpchain",)] = vq
                    if debug:
                        st[("r2d", t)] = emit(
                            "g", "dma_start",
                            dbg["res2"][:, 1024 * t:1024 * t + 1024],
                            stream_t[t], w=[(sem["V"], vq)], i=[("DBG", 16)])
                return f

            def mk_ln2_apply(t, rk):
                def f():
                    wts = [(sem["S"], st[("rstd", rk)])]
                    if debug:
                        wts.append((sem["DBG"], st[("r2d", t)]))
                    vv = emit("v", "tensor_scalar", stream_t[t], stream_t[t],
                              lnm16[:, t:t + 1], lnr16[:, t:t + 1],
                              ALU.subtract, ALU.mult,
                              w=wts,
                              i=[("V", 1)])
                    q = "s" if (half == 1 and t % 2 == 1) else "y"
                    emit(q, "dma_start",
                         ap(d_out, 1024 * 128 * t, [[1024, 128], [1, 1024]]),
                         stream_t[t],
                         w=[(sem["V"], vv)], i=[("LO", 16)])
                return f

            def projdone():
                st[("projdone",)] = ctr["T"]

            for t in range(t0, t0 + 8):
                th.append((1, mk_proj(t, 0)))
                th.append((1, mk_proj(t, 1)))
                th.append((0, mk_ln1_stats(t)))
            th.append((0, projdone))
            th.append((0, mk_ln_batch("ln1", t0)))
            for t in range(t0, t0 + 8):
                th.append((0, mk_ln1_apply(t)))
            for t in range(t0, t0 + 8):
                th.append((1, mk_tr(t, 0)))
                th.append((1, mk_tr(t, 1)))
            for t in range(t0, t0 + 8):
                th.append((1, mk_router(t)))
            th.append((0, gating))
            for ci in range(4):
                th.append((0, mk_ewload(0, 0, ci)))
            for e in range(NEXP):
                for hh in range(2):
                    for t in range(t0, t0 + 8):
                        th.append((1, mk_exp(e, t, hh)))
                        if not (e == NEXP - 1 and hh == 1) and t - t0 >= 4:
                            ne, nh = (e, 1) if hh == 0 else (e + 1, 0)
                            th.append((0, mk_ewload(ne, nh, t - t0 - 4)))
            tail_th = []
            for g4 in range(2):
                b0 = t0 + 4 * g4
                for t in range(b0, b0 + 4):
                    th.append((0, mk_ln2_stats(t)))
                rk = "ln2" + ("a" if g4 == 0 else "b")
                th.append((0, mk_ln_batch(rk, b0, 4)))
                for t in range(b0, b0 + 4):
                    (tail_th if half == 0 else th).append(
                        (0, mk_ln2_apply(t, rk)))
            return th, st, tail_th

        seq = [(hp, p) for p in (0, 1) for hp in range(NHP)]
        kvq = iter(kv_groups(0, 0, 0))
        for g in kvq:
            g()
        st0 = None
        spb_user = {0: 0, 1: 0}
        for si, (hp, p) in enumerate(seq):
            kv_done = [False]
            if 1 <= si <= 4:
                load_wpT_chunk(si - 1)
            if si == 2:
                ld_stream1 = emit("y", "dma_start", str8,
                                  ap(d_xot, 0,
                                     [[1024, 128], [1024 * 128, 8], [1, 1024]]),
                                  w=[(sem["T"], guard[7])], i=[("LSTR", 16)])
            if p == 1 and hp == 0:
                th0, st0, tail0 = post_thunks(0)
                st0["ld_stream"] = ld_stream1
                work_q.extend(th0)
            nxt = seq[si + 1] if si + 1 < len(seq) else None
            kvq = iter(kv_groups(nxt[0], nxt[1], si + 1)) if nxt else iter(())
            buf = hp % 2
            kbase, vbase = 4096 * buf, 4160 * buf
            kr = kv_reload.get((hp, p))
            for slot in PASS_SLOTS[p]:
                npair = PAIRS[slot]
                for j in range(npair):
                    wts = []
                    if s_exp_prev:
                        wts.append((sem["S"], s_exp_prev))
                    else:
                        wts.append((sem["S"], qall_s[NHP - 1]))
                    tcn = (2 * j + 1) // 4
                    if p == 0 or tcn >= 4:
                        wts.append((sem["V"], kv_kevac[(hp, p)][tcn]))
                    if slot == PASS_SLOTS[p][0] and j == 0 and kr:
                        wts.append((sem[kr[0]], kr[1]))
                    t_sc = 0
                    for kk in range(2):
                        kgi = 2 * j + kk
                        for h in range(2):
                            r0, r1 = (0, 64) if h == 0 else (64, 128)
                            t_sc = emit(
                                "t", "matmul",
                                psS[:, 1024 * kk + 512 * h:1024 * kk + 512 * h + 512],
                                kb[r0:r1, kbase + 128 * kgi:kbase + 128 * kgi + 128],
                                qall[r0:r1, 2048 * hp + 512 * slot:2048 * hp + 512 * slot + 512],
                                start=True, stop=True,
                                w=(wts if (kk == 0 and h == 0) else ()),
                                i=([("T", 1)] if (kk == 1 and h == 1) else ()))
                    if len(av_queue) >= 2:
                        pop_av()
                    if not (slot == PASS_SLOTS[p][0] and j == 0):
                        for _ in range(2):
                            ng = next(kvq, None)
                            if ng is not None:
                                ng()
                            elif not kv_done[0]:
                                kv_done[0] = True
                                if si + 2 < len(seq):
                                    load_wq_usage(8 + si + 2,
                                                  kv_tlast[seq[si]])
                                    nhp, npp = seq[si + 2]
                                    for tc in ((0, 1) if npp == 0 else (4, 5)):
                                        if (npp, nhp, tc) not in xk_load:
                                            load_xk(npp, nhp, tc)
                    heavy_done = 0
                    pops = 0
                    if epi_q:
                        epi_q.pop(0)()
                        epi_pops[0] += 1
                    while (work_q and heavy_done < 1 and pops < 3
                           and epi_pops[0] >= P0_EPI):
                        hv, fn = work_q.pop(0)
                        fn()
                        pops += 1
                        heavy_done += hv
                    par = gpair[0] % 2
                    gpair[0] += 1
                    wexp = [(sem["T"], t_sc)]
                    s_exp_prev = emit("s", "activation", t_exp,
                                      psS[:, 0:2048], AF.Exp,
                                      w=wexp, i=[("S", 1)], scale=0.125)
                    wln = [(sem["S"], s_exp_prev)]
                    if spb_user[0]:
                        wln.append((sem["V"], spb_user[0]))
                    s_ln_prev = emit("s", "activation", spb, t_exp, AF.Ln,
                                     w=wln, i=[("S", 1)], bias=1.0)
                    v_sq_raw = emit("v", "tensor_tensor",
                                    pbb[:, 2048 * par:2048 * par + 2048],
                                    spb, spb, ALU.mult,
                                    w=[(sem["S"], s_ln_prev),
                                       (sem["T"], t_av_prev)], i=[("V", 1)])
                    spb_user[0] = v_sq_raw
                    v_sq_prev = v_sq_raw
                    if j >= npair - 4:
                        vw = [(sem["V"], v_sq_raw)]
                        if not mask_waited[0]:
                            vw.append((sem["LSC"], ld_mask))
                            mask_waited[0] = True
                        m0 = 2 * (j - (npair - 4))
                        v_sq_prev = emit(
                            "v", "tensor_tensor",
                            ap(pbb, 2048 * par,
                               [[4096, 128], [1024, 2], [512, 2], [1, 512]]),
                            ap(pbb, 2048 * par,
                               [[4096, 128], [1024, 2], [512, 2], [1, 512]]),
                            ap(mkb, 512 * m0,
                               [[4096, 128], [512, 2], [0, 2], [1, 512]]),
                            ALU.mult, w=vw, i=[("V", 1)])
                    av_queue.append((slot, j, v_sq_prev, par, hp, p,
                                     j == npair - 1))
            for g in kvq:
                g()
            if not kv_done[0] and si + 2 < len(seq):
                load_wq_usage(8 + si + 2, kv_tlast[seq[si + 1]])
                nhp, npp = seq[si + 2]
                for tc in ((0, 1) if npp == 0 else (4, 5)):
                    if (npp, nhp, tc) not in xk_load:
                        load_xk(npp, nhp, tc)
        while av_queue:
            pop_av()
        while epi_q:
            epi_q.pop(0)()
        while work_q:
            work_q.pop(0)[1]()
        for _hv, f in tail0:
            f()

        if debug:
            emit("g", "dma_start", dbg["attn"], attn,
                 w=[(sem["V"], ctr["V"]), (sem["T"], ctr["T"])],
                 i=[("DBG", 16)])
            emit("g", "dma_start", dbg["kb"], kb, i=[("DBG", 16)])
            emit("g", "dma_start", dbg["vb"], vb, i=[("DBG", 16)])
            emit("g", "dma_start", dbg["qall"], qall, i=[("DBG", 16)])

        th1, st1, tail1 = post_thunks(1)
        w_s2 = [(sem["T"], ctr["T"]), (sem["V"], ctr["V"]),
                (sem["S"], ctr["S"])]
        emit("y", "dma_start", str4a,
             ap(d_xot, 8 * 128 * 1024,
                [[1024, 128], [1024 * 128, 4], [1, 1024]]),
             w=w_s2, i=[("LSTR", 16)])
        emit("y", "dma_start", str4b,
             ap(d_xot, 12 * 128 * 1024,
                [[1024, 128], [1024 * 128, 4], [1, 1024]]),
             w=w_s2, i=[("LSTR", 16)])
        st1["ld_stream"] = ctr["LSTR"]
        for _hv, f in th1:
            f()
        if debug:
            emit("g", "dma_start", dbg["h1fm"], h1fm,
                 w=[(sem["V"], ctr["V"])], i=[("DBG", 16)])
            emit("g", "dma_start", dbg["gates"], gates, i=[("DBG", 16)])
            emit("g", "dma_start", dbg["bias16"], g_t, i=[("DBG", 16)])
        emit("y", "wait_ge", sem["LO"], ctr["LO"])

        block = En(nc.Block())

        def runner(key):
            def run(eng):
                for waits, method, args, kw, incs in recs[key]:
                    for s, val in waits:
                        if val:
                            eng.wait_ge(s, val)
                    if method == "wait_ge":
                        eng.wait_ge(*args)
                        continue
                    instr = getattr(eng, method)(*args, **kw)
                    for nm, n in incs:
                        instr.then_inc(sem[nm], n)
            return run

        block.tensor(runner("t"))
        block.scalar(runner("s"))
        block.vector(runner("v"))
        block.gpsimd(runner("g"))
        block.sync(runner("y"))

    return nc



def _role_masks(role):
    q = np.arange(QS)[None, :]
    k = np.arange(128)[:, None]
    tiles = []
    for j in range(8):
        if role == 0:
            m = (np.ones((128, QS), np.float32) if j < 4
                 else (q >= 128 * (j - 4) + k).astype(np.float32))
        else:
            m = ((q >= 128 * j + k).astype(np.float32) if j < 4
                 else np.zeros((128, QS), np.float32))
        tiles.append(m)
    return np.ascontiguousarray(
        np.concatenate(tiles, axis=1).astype(ml_dtypes.bfloat16))


_NC_CACHE = {}


def kernel(L0_attn_x, L0_attn_w_qkv, L0_attn_w_proj, L0_attn_b_proj,
           L0_ffn_router_w, L0_ffn_expert_weights):
    x = np.asarray(L0_attn_x, np.float32)
    w_qkv = np.asarray(L0_attn_w_qkv, np.float32)
    w_proj = np.asarray(L0_attn_w_proj, np.float32)
    b_proj = np.asarray(L0_attn_b_proj, np.float32)
    router_w = np.asarray(L0_ffn_router_w, np.float32)
    ew = np.ascontiguousarray(np.asarray(L0_ffn_expert_weights, np.float32))

    wq8 = np.empty((NHP, D, 384), np.float32)
    for hp in range(NHP):
        rows = np.r_[128 * hp:128 * hp + 128,
                     1024 + 128 * hp:1024 + 128 * hp + 128,
                     2048 + 128 * hp:2048 + 128 * hp + 128]
        wq8[hp] = w_qkv[rows, :].T
    wpT = np.ascontiguousarray(w_proj.T)
    rwT = np.ascontiguousarray(router_w.T)
    bp = np.ascontiguousarray(b_proj[None, :])
    eye = np.eye(128, dtype=np.float32)
    masks = {r: _role_masks(r) for r in (0, 1)}

    in_maps = []
    blocks = {}
    for core in range(8):
        b, role = core // 2, core % 2
        gs = [e - 512 for e in ROLE_EXT[role]]
        blocks[core] = gs
        own = np.concatenate([np.arange(g, g + 512) for g in gs])
        xb = x[b]
        in_maps.append({
            "xkf": np.ascontiguousarray(xb.T),
            "xqf": np.ascontiguousarray(xb[own].T),
            "xot": np.ascontiguousarray(xb[own]),
            "wq8": wq8, "wpT": wpT, "bp": bp, "rwT": rwT, "ew": ew,
            "mk": masks[role], "eyein": eye,
        })

    if "nc" not in _NC_CACHE:
        _NC_CACHE["nc"] = build_program()
    nc = _NC_CACHE["nc"]
    res = run_bass_kernel_spmd(nc, in_maps, core_ids=list(range(8)))
    out = np.empty((B, T, D), np.float32)
    for core in range(8):
        co = np.asarray(res.results[core]["out"], np.float32)
        b = core // 2
        for s, g in enumerate(blocks[core]):
            out[b, g:g + 512] = co[512 * s:512 * (s + 1)]
    return out


# version 28
# speedup vs baseline: 1.0401x; 1.0401x over previous
import contextlib
import numpy as np
import ml_dtypes
import concourse.bass as bass
import concourse.mybir as mybir
from concourse.bass_utils import run_bass_kernel_spmd

F32 = mybir.dt.float32
BF16 = mybir.dt.bfloat16
FP8 = mybir.dt.float8e4
AF = mybir.ActivationFunctionType
ALU = mybir.AluOpType
AX = mybir.AxisListType.X

B, T, D = 4, 4096, 1024
NHP = 8
NSLOT, QS = 4, 512
OWN = NSLOT * QS
NTT = OWN // 128
ENV = [1024, 2048, 3072, 4096]
KG = [e // 128 for e in ENV]
PAIRS = [k // 2 for k in KG]
NEXP = 8
ROLE_EXT = {0: [1024, 2048, 3072, 4096], 1: [512, 1536, 2560, 3584]}
PASS_SLOTS = {0: (0, 1), 1: (2, 3)}

SB0 = 17408
OFF_XKB = SB0 + 0
OFF_KB = SB0 + 16384
OFF_VB = SB0 + 32768
OFF_STR = SB0 + 49408
OFF_ATTN = SB0 + 82176
OFF_QALL = SB0 + 114944
OFF_WPT = SB0 + 147712
OFF_T = SB0 + 164096
OFF_SP = SB0 + 168192
OFF_P = SB0 + 172288
OFF_RS = SB0 + 180480
OFF_WQ = SB0 + 184576
OFF_MASK = SB0 + 192768
OFF_RS1 = SB0 + 200960
OFF_EYE = SB0 + 205056
OFF_RWB = SB0 + 205568
OFF_GATES = SB0 + 206080
OFF_LOG = SB0 + 206592
OFF_PROBS = SB0 + 207104
OFF_G16 = SB0 + 207616
OFF_LN16 = SB0 + 207872
OFF_BPB = SB0 + 208128
OFF_ONES = SB0 + 210176
OFF_BC = SB0 + 210432
assert OFF_BC + 1024 <= 229344

STREAM_OFF = {}
for _t in range(8):
    STREAM_OFF[_t] = OFF_STR + 4096 * _t
for _t in range(8, 12):
    STREAM_OFF[_t] = OFF_XKB + 4096 * (_t - 8)
for _t in range(12, 16):
    STREAM_OFF[_t] = OFF_T + 4096 * (_t - 12)

DEBUG = False


def build_program(debug=DEBUG):
    nc = bass.Bass("TRN2", target_bir_lowering=False, debug=False)

    d_xkf = nc.dram_tensor("xkf", [D, T], F32, kind="ExternalInput").ap()
    d_xqf = nc.dram_tensor("xqf", [D, OWN], F32, kind="ExternalInput").ap()
    d_xot = nc.dram_tensor("xot", [OWN, D], F32, kind="ExternalInput").ap()
    d_wq8 = nc.dram_tensor("wq8", [NHP, D, 384], F32, kind="ExternalInput").ap()
    d_wpT = nc.dram_tensor("wpT", [D, D], F32, kind="ExternalInput").ap()
    d_bp = nc.dram_tensor("bp", [1, D], F32, kind="ExternalInput").ap()
    d_rwT = nc.dram_tensor("rwT", [D, NEXP], F32, kind="ExternalInput").ap()
    d_ew = nc.dram_tensor("ew", [NEXP, D, D], F32, kind="ExternalInput").ap()
    d_mk = nc.dram_tensor("mk", [128, 8 * QS], BF16, kind="ExternalInput").ap()
    d_eye = nc.dram_tensor("eyein", [128, 128], F32, kind="ExternalInput").ap()
    d_out = nc.dram_tensor("out", [OWN, D], F32, kind="ExternalOutput").ap()
    d_scr = nc.dram_tensor("scr", [2 * 8, QS], F32, kind="Internal").ap()
    d_kvs = nc.dram_tensor("kvs", [NHP, 128, 4128], BF16, kind="Internal").ap()
    dbg = {}
    if debug:
        for nm, shp in [("qall", [128, 16384]), ("kb", [128, 8192]),
                        ("vb", [128, 8320]), ("attn", [128, 16384]),
                        ("h1fm", [128, 16384]), ("gates", [128, 128]),
                        ("bias16", [128, 16]), ("res2", [128, 16384])]:
            dbg[nm] = nc.dram_tensor("dbg_" + nm, shp, F32,
                                     kind="ExternalOutput").ap()

    A = nc.alloc_sbuf_tensor_at
    xkb = [A("xkb%d" % i, [128, 4096], BF16, offset=OFF_XKB + 8192 * i).ap()
           for i in range(2)]
    kb = A("kb", [128, 8192], BF16, offset=OFF_KB).ap()
    vb = A("vb", [128, 8320], BF16, offset=OFF_VB).ap()
    str8 = A("str8", [128, 8192], F32, offset=OFF_STR).ap()
    str4a = A("str4a", [128, 4096], F32, offset=OFF_XKB).ap()
    str4b = A("str4b", [128, 4096], F32, offset=OFF_T).ap()
    stream_t = [A("stream%d" % t, [128, 1024], F32, offset=STREAM_OFF[t]).ap()
                for t in range(16)]
    attn = A("attn", [128, 16384], BF16, offset=OFF_ATTN).ap()
    h1fm = A("h1fm", [128, 16384], BF16, offset=OFF_ATTN).ap()
    qall = A("qall", [128, 16384], BF16, offset=OFF_QALL).ap()
    wpT = A("wpT", [128, 8192], BF16, offset=OFF_WPT).ap()
    ewh2 = A("ewh2", [128, 8192], BF16, offset=OFF_WPT).ap()
    xq0 = A("xq0", [128, 8192], BF16, offset=OFF_T).ap()
    xq1 = A("xq1", [128, 8192], BF16, offset=OFF_ATTN).ap()
    t_exp = A("t_exp", [128, 2048], BF16, offset=OFF_T).ap()
    spb = A("spb", [128, 2048], BF16, offset=OFF_SP).ap()
    pbb = A("pbb", [128, 4096], BF16, offset=OFF_P).ap()
    bcb = A("bcb", [128, 512], BF16, offset=OFF_BC).ap()
    rsb2 = [A("rsb0", [128, 1024], F32, offset=OFF_RS).ap(),
            A("rsb1", [128, 1024], F32, offset=OFF_RS1).ap()]
    wqK = [A("wqK%d" % i, [128, 2048], BF16, offset=OFF_WQ + 4096 * i).ap()
           for i in range(2)]
    wqA = [A("wqA%d" % i, [128, 3072], BF16, offset=OFF_STR + 6144 * i).ap()
           for i in range(2)]
    mkb = A("mkb", [128, 4096], BF16, offset=OFF_MASK).ap()
    eyeb = A("eyeb", [128, 128], F32, offset=OFF_EYE).ap()
    rwb = A("rwb", [128, 64], BF16, offset=OFF_RWB).ap()
    gates = A("gates", [128, 128], F32, offset=OFF_GATES).ap()
    logit = A("logit", [128, 128], F32, offset=OFF_LOG).ap()
    probs = A("probs", [128, 128], F32, offset=OFF_PROBS).ap()
    g_v1 = A("g_v1", [128, 16], F32, offset=OFF_G16).ap()
    g_v2 = A("g_v2", [128, 16], F32, offset=OFF_G16 + 64).ap()
    g_rd = A("g_rd", [128, 16], F32, offset=OFF_G16 + 128).ap()
    g_t = A("g_t", [128, 16], F32, offset=OFF_G16 + 192).ap()
    lnm16 = A("lnm16", [128, 16], F32, offset=OFF_LN16).ap()
    lnq16 = A("lnq16", [128, 16], F32, offset=OFF_LN16 + 64).ap()
    lnv16 = A("lnv16", [128, 16], F32, offset=OFF_LN16 + 128).ap()
    lnr16 = A("lnr16", [128, 16], F32, offset=OFF_LN16 + 192).ap()
    bpb = A("bpb", [1, 1024], BF16, offset=OFF_BPB).ap()
    onesr = A("onesr", [1, 128], BF16, offset=OFF_ONES).ap()

    def ap(base, off, dims):
        return bass.AP(base.tensor, off, dims)

    recs = {k: [] for k in "tsvgy"}
    SEMN = ("T", "S", "V", "LO", "XS", "XS1", "XS2", "XS3",
            "WQ0", "WQ1", "XB0", "XB1",
            "KS0", "KS1", "KL0", "KL1", "SCR0", "SCR1", "BC", "EW0", "EW1",
            "WPT", "SET", "LSC", "EYE", "LSTR", "DBG")
    ctr = {k: 0 for k in SEMN}

    with contextlib.ExitStack() as ctx:
        En = ctx.enter_context
        psS = En(nc.psum_tensor("psS", [128, 2048], F32)).ap()
        psA = En(nc.psum_tensor("psA", [128, 512], F32)).ap()
        psB = En(nc.psum_tensor("psB", [128, 512], F32)).ap()
        psK = En(nc.psum_tensor("psK", [128, 512], F32)).ap()
        psV = En(nc.psum_tensor("psV", [128, 512], F32)).ap()
        sem = {n: En(nc.semaphore("s" + n)) for n in SEMN}

        def emit(e, method, *args, w=(), i=(), **kw):
            out = []
            for nm, n in i:
                ctr[nm] += n
                out.append(ctr[nm])
            recs[e].append((tuple(w), method, args, kw, tuple(i)))
            return out[0] if len(out) == 1 else (out or 0)

        ld_mask = emit("y", "dma_start", mkb, d_mk, i=[("LSC", 16)])
        ld_eye = emit("y", "dma_start", eyeb, d_eye, i=[("EYE", 16)])
        ld_stream1 = 0
        ld_xqp = []

        def load_xq_parts():
            for part in range(4):
                xdst = xq0 if part < 2 else xq1
                sn = "XS" if part == 0 else "XS%d" % part
                ld_xqp.append((sn, emit(
                    "g", "dma_start",
                    ap(xdst, 4096 * (part % 2),
                       [[8192, 128], [2048, 2], [1, 2048]]),
                    ap(d_xqf, 256 * OWN * part,
                       [[2048, 128], [2048 * 128, 2], [1, 2048]]),
                    i=[(sn, 16)])))
            emit("g", "dma_start", bpb, d_bp, i=[("SET", 16)])
            return emit("g", "dma_start", rwb,
                        ap(d_rwT, 0, [[8, 128], [8 * 128, 8], [1, 8]]),
                        i=[("SET", 16)])
        ld_wpT = 4 * 16

        def load_wpT_chunk(c):
            emit("g", "dma_start",
                 ap(wpT, 2048 * c, [[8192, 128], [1024, 2], [1, 1024]]),
                 ap(d_wpT, 131072 * 2 * c, [[1024, 128], [131072, 2], [1, 1024]]),
                 i=[("WPT", 16)])
        emit("v", "memset",
             ap(vb, 64, [[8320, 128], [4160, 2], [130, 32], [65, 2]]), 1.0,
             i=[("V", 1)])
        emit("v", "memset", onesr, 1.0, i=[("V", 1)])
        emit("v", "memset", rsb2[0], 1.0, i=[("V", 1)])
        v_ones = emit("v", "memset", rsb2[1], 1.0, i=[("V", 1)])

        ld_wq = {}

        def load_wq_usage(k, guard_t):
            hp = k % 8
            qn = "WQ%d" % (k % 2)
            wts = [(sem["T"], guard_t)] if guard_t else []
            if k < 8:
                dst = wqA[k % 2]
                src = ap(d_wq8, hp * D * 384,
                         [[384, 128], [384 * 128, 8], [1, 384]])
            else:
                dst = wqK[k % 2]
                src = ap(d_wq8, hp * D * 384 + 128,
                         [[384, 128], [384 * 128, 8], [1, 256]])
            ld_wq[k] = (qn, emit("g", "dma_start", dst, src,
                                 w=wts, i=[(qn, 16)]))

        load_wq_usage(0, 0)
        load_wq_usage(1, 0)
        ld_small = load_xq_parts()

        xk_load = {}
        xkb_guard = {0: 0, 1: 0}

        def load_xk(p, hp, tc):
            bi = tc % 2
            qn = "XB%d" % bi
            wts = [(sem["T"], xkb_guard[bi])] if xkb_guard[bi] else []
            xk_load[(p, hp, tc)] = (qn, emit(
                "g", "dma_start", xkb[bi],
                ap(d_xkf, 512 * tc, [[4096, 128], [4096 * 128, 8], [1, 512]]),
                w=wts, i=[(qn, 16)]))

        load_xk(0, 0, 0)
        load_xk(0, 0, 1)

        evac_rot = {r: (sem["S"], 0) for r in range(4)}
        qall_s = {}
        guard = {}
        for hp in range(NHP):
            for qt in range(NSLOT):
                gi = hp * NSLOT + qt
                rot = gi % 4
                ps = psS[:, 512 * rot:512 * rot + 512]
                wsem, wval = evac_rot[rot]
                qn, qv = ld_wq[hp]
                wts = [(sem[qn], qv),
                       (sem[ld_xqp[0][0]], ld_xqp[0][1])]
                if wval:
                    wts.append((wsem, wval))
                lq = 0
                for c in range(8):
                    xsrc = xq0 if c < 4 else xq1
                    cc = c if c < 4 else c - 4
                    if c == 0:
                        wc = wts
                    elif c in (2, 4, 6):
                        wc = [(sem[ld_xqp[c // 2][0]], ld_xqp[c // 2][1])]
                    else:
                        wc = ()
                    lq = emit(
                        "t", "matmul", ps,
                        wqA[hp % 2][:, 384 * c:384 * c + 128],
                        xsrc[:, 2048 * cc + 512 * qt:2048 * cc + 512 * qt + 512],
                        start=(c == 0), stop=(c == 7),
                        w=wc,
                        i=([("T", 1)] if c == 7 else ()))
                col = 2048 * hp + 512 * qt
                sv = emit("s", "copy", qall[:, col:col + 512],
                          ps, w=[(sem["T"], lq)], i=[("S", 1)])
                evac_rot[rot] = (sem["S"], sv)
                qall_s[hp] = sv
                guard[hp] = lq
            if hp + 2 < NHP:
                load_wq_usage(hp + 2, guard[hp])
        load_wq_usage(8, guard[6])
        load_wq_usage(9, guard[7])

        bg_rot = {0: (sem["S"], 0), 1: (sem["S"], 0)}
        bg_i = [0]

        def bg_bank():
            r = bg_i[0] % 2
            bg_i[0] += 1
            return r, (psK if r == 0 else psV)

        kv_tlast = {}
        kv_kevac = {}
        kv_vevac = {}
        kv_reload = {}
        ks_val = {0: 0, 1: 0}
        t_av_box = [0]

        def kv_groups(hp, p, si):
            buf = hp % 2
            kbase, vbase = 4096 * buf, 4160 * buf
            use = 8 + si
            tcs = (0, 1, 2, 3) if p == 0 else (4, 5, 6, 7)
            kv_kevac[(hp, p)] = {}
            kv_vevac[(hp, p)] = {}
            if p == 1:
                kn, sn = "KL%d" % buf, "KS%d" % buf
                w_r = [(sem["T"], t_av_box[0]), (sem[sn], ks_val[buf])]
                emit("y", "dma_start", kb[:, kbase:kbase + 2048],
                     ap(d_kvs, hp * 128 * 4128, [[4128, 128], [1, 2048]]),
                     w=w_r, i=[(kn, 16)])
                kl = emit("y", "dma_start", vb[:, vbase:vbase + 2080],
                          ap(d_kvs, hp * 128 * 4128 + 2048,
                             [[4128, 128], [1, 2080]]),
                          w=w_r, i=[(kn, 16)])
                kv_reload[(hp, p)] = (kn, kl)
            for tc in tcs[:2]:
                if (p, hp, tc) not in xk_load:
                    load_xk(p, hp, tc)
            for tc in tcs:
                bi = tc % 2
                qx, vx = xk_load[(p, hp, tc)]
                qn, qv = ld_wq[use]

                def gK(tc=tc, bi=bi, qx=qx, vx=vx, qn=qn, qv=qv,
                       kbase=kbase, buf=buf, p=p, hp=hp):
                    r, psx = bg_bank()
                    wsem, wval = bg_rot[r]
                    wts = [(sem[qn], qv), (sem[qx], vx)]
                    if wval:
                        wts.append((wsem, wval))
                    tm = 0
                    for c in range(8):
                        tm = emit("t", "matmul", psx[:, 0:512],
                                  wqK[buf][:, 256 * c:256 * c + 128],
                                  xkb[bi][:, 512 * c:512 * c + 512],
                                  start=(c == 0), stop=(c == 7),
                                  w=(wts if c == 0 else ()),
                                  i=([("T", 1)] if c == 7 else ()))
                    wev = [(sem["T"], tm)]
                    if p == 0 and ks_val[buf]:
                        wev.append((sem["KS%d" % buf], ks_val[buf]))
                    vv = emit("v", "tensor_copy",
                              kb[:, kbase + 512 * tc:kbase + 512 * tc + 512],
                              psx[:, 0:512], w=wev, i=[("V", 1)])
                    bg_rot[r] = (sem["V"], vv)
                    kv_tlast[(hp, p)] = tm
                    kv_kevac[(hp, p)][tc] = vv
                yield gK
                for i4 in range(4):
                    def gV(tc=tc, bi=bi, i4=i4, qx=qx, vx=vx, qn=qn, qv=qv,
                           kbase=kbase, vbase=vbase, buf=buf, p=p, hp=hp,
                           tcs=tcs):
                        tv = 4 * tc + i4
                        r, psx = bg_bank()
                        wsem, wval = bg_rot[r]
                        wts = [(sem[qn], qv), (sem[qx], vx)]
                        if wval:
                            wts.append((wsem, wval))
                        tm = 0
                        for c in range(8):
                            tm = emit(
                                "t", "matmul", psx[:, 0:128],
                                xkb[bi][:, 512 * c + 128 * i4:512 * c + 128 * i4 + 128],
                                wqK[buf][:, 256 * c + 128:256 * c + 256],
                                start=(c == 0), stop=(c == 7),
                                w=(wts if c == 0 else ()),
                                i=([("T", 1)] if c == 7 else ()))
                        wev = [(sem["T"], tm)]
                        if p == 0 and ks_val[buf]:
                            wev.append((sem["KS%d" % buf], ks_val[buf]))
                        vv = emit("v", "tensor_copy",
                                  ap(vb, vbase + 130 * tv,
                                     [[8320, 128], [65, 2], [1, 64]]),
                                  ap(psK if r == 0 else psV, 0,
                                     [[512, 128], [64, 2], [1, 64]]),
                                  w=wev, i=[("V", 1)])
                        bg_rot[r] = (sem["V"], vv)
                        kv_tlast[(hp, p)] = tm
                        kv_vevac[(hp, p)][tv] = vv
                        if i4 == 3:
                            xkb_guard[bi] = tm
                            if tc + 2 in tcs:
                                load_xk(p, hp, tc + 2)
                            if p == 0 and tc == tcs[-1]:
                                sn = "KS%d" % buf
                                wsp = [(sem["V"], vv)]
                                emit("y", "dma_start",
                                     ap(d_kvs, hp * 128 * 4128,
                                        [[4128, 128], [1, 2048]]),
                                     kb[:, kbase:kbase + 2048],
                                     w=wsp, i=[(sn, 16)])
                                ks_val[buf] = emit(
                                    "y", "dma_start",
                                    ap(d_kvs, hp * 128 * 4128 + 2048,
                                       [[4128, 128], [1, 2080]]),
                                    vb[:, vbase:vbase + 2080],
                                    w=wsp, i=[(sn, 16)])
                    yield gV

        s_exp_prev = 0
        s_ln_prev = 0
        v_sq_prev = 0
        t_av_prev = ctr["T"]
        av_evac_prev = 0
        v_apply_last = 0
        lg_scr = {0: 0, 1: 0}
        rs_copy_last = 0
        av_queue = []
        epi_q = []
        work_q = []
        hp_start_v_cur = [0]
        gpair = [0]
        mask_waited = [False]
        epi_pops = [0]
        v_cool = [0]
        P0_EPI = NHP * 9

        def emit_av(slot, j, hp_, p_, v_sq, par):
            nonlocal t_av_prev
            nkg = KG[slot]
            vbase = 4160 * (hp_ % 2)
            pb0 = 2048 * par
            for kk in range(2):
                kgi = 2 * j + kk
                for h in range(2):
                    ps = psA if h == 0 else psB
                    wts = []
                    if kk == 0 and h == 0:
                        wts.append((sem["V"], v_sq))
                        tvn = 2 * j + 1
                        if p_ == 0 or tvn >= 16:
                            wts.append((sem["V"], kv_vevac[(hp_, p_)][tvn]))
                        if kgi == 0 and av_evac_prev:
                            wts.append((sem["V"], av_evac_prev))
                    t_av_prev = emit(
                        "t", "matmul", ps[0:65, :],
                        vb[:, vbase + 130 * kgi + 65 * h:vbase + 130 * kgi + 65 * h + 65],
                        pbb[:, pb0 + 1024 * kk + 512 * h:pb0 + 1024 * kk + 512 * h + 512],
                        start=(kgi == 0), stop=(kgi == nkg - 1),
                        w=wts, i=([("T", 1)] if (kk == 1 and h == 1) else ()))
            t_av_box[0] = t_av_prev

        def pop_av():
            nonlocal av_evac_prev, rs_copy_last
            slot_, j_, vsq_, par_, hp_, p_, last_ = av_queue.pop(0)
            emit_av(slot_, j_, hp_, p_, vsq_, par_)
            if last_:
                p_ = hp_ % 2
                rsb = rsb2[p_]
                acol = 2048 * hp_ + 512 * slot_
                emit("v", "tensor_copy", attn[0:64, acol:acol + 512],
                     psA[0:64, :], w=[(sem["T"], t_av_prev)], i=[("V", 1)])
                emit("v", "tensor_copy", attn[64:128, acol:acol + 512],
                     psB[0:64, :], i=[("V", 1)])
                kA, kB = 2 * slot_, 2 * slot_ + 1
                rs_w = [(sem["SCR%d" % p_], lg_scr[p_])] if (
                    slot_ % 2 == 0 and lg_scr[p_]) else []
                emit("v", "tensor_copy",
                     rsb[32 * (kA % 4):32 * (kA % 4) + 1,
                         512 * (kA // 4):512 * (kA // 4) + 512],
                     psA[64:65, :], w=rs_w, i=[("V", 1)])
                av_evac_prev = emit("v", "tensor_copy",
                                    rsb[32 * (kB % 4):32 * (kB % 4) + 1,
                                        512 * (kB // 4):512 * (kB // 4) + 512],
                                    psB[64:65, :], i=[("V", 1)])
                rs_copy_last = av_evac_prev
                if slot_ % 2 == 1:
                    queue_epilogue(hp_, slot_ // 2)

        class EpiState:
            vv = 0
            v_recip = 0

        def queue_epilogue(hp_, pa):
            es = EpiState()
            p_ = hp_ % 2
            rsb = rsb2[p_]
            scr_n = "SCR%d" % p_
            ch = 512 * pa

            def t_eps():
                es.vv = emit("v", "tensor_scalar", rsb[:, ch:ch + 512],
                             rsb[:, ch:ch + 512], 1e-9, None,
                             ALU.add, w=[(sem["V"], rs_copy_last)],
                             i=[("V", 1)])

            def mk_recip(k4):
                def t_recip():
                    es.v_recip = emit("v", "reciprocal",
                                      rsb[:, 256 * k4:256 * k4 + 256],
                                      rsb[:, 256 * k4:256 * k4 + 256],
                                      w=[(sem["V"], es.vv)], i=[("V", 1)])
                return t_recip

            def mk_scr(k):
                def t_scr():
                    scr_w = [(sem["V"], es.v_recip)]
                    if v_apply_last:
                        scr_w.append((sem["V"], v_apply_last))
                    lg_scr[p_] = emit(
                        "g", "dma_start",
                        ap(d_scr, (p_ * 8 + k) * QS, [[512, 1], [1, 512]]),
                        rsb[32 * (k % 4):32 * (k % 4) + 1,
                            512 * (k // 4):512 * (k // 4) + 512],
                        w=scr_w, i=[(scr_n, 16)])
                return t_scr

            def mk_bc(s2):
                def t_bc():
                    nonlocal v_apply_last
                    lg2 = 0
                    for h in range(2):
                        bc_w = [(sem[scr_n], lg_scr[p_])]
                        if v_apply_last:
                            bc_w.append((sem["V"], v_apply_last))
                        lg2 = emit("g", "dma_start",
                                   bcb[64 * h:64 * h + 64, :],
                                   ap(d_scr, (p_ * 8 + 2 * s2 + h) * QS,
                                      [[0, 64], [1, 512]]),
                                   w=(bc_w if h == 0 else ()),
                                   i=[("BC", 16)])
                    acol = 2048 * hp_ + 512 * s2
                    v_apply_last = emit(
                        "v", "tensor_tensor", attn[:, acol:acol + 512],
                        attn[:, acol:acol + 512], bcb[:, :], ALU.mult,
                        w=[(sem["BC"], lg2)], i=[("V", 1)])
                return t_bc

            epi_q.append(t_eps)
            for k4 in (2 * pa, 2 * pa + 1):
                epi_q.append(mk_recip(k4))
            for k in range(4 * pa, 4 * pa + 4):
                epi_q.append(mk_scr(k))
            for s2 in (2 * pa, 2 * pa + 1):
                epi_q.append(mk_bc(s2))

        def post_thunks(half):
            t0 = 8 * half
            th = []
            st = {}

            def ew_slice(c, hh):
                if half == 0:
                    return qall[:, 2048 * c + 512 * hh:2048 * c + 512 * hh + 512]
                return ewh2[:, 1024 * c + 512 * hh:1024 * c + 512 * hh + 512]

            def dump_dst():
                return qall[:, 0:1024] if half == 0 else qall[:, 1024:2048]

            def mk_proj(t, hh):
                def f():
                    r, psx = bg_bank()
                    wsem, wval = bg_rot[r]
                    wts = [(sem["WPT"], ld_wpT), (sem["V"], v_apply_last),
                           (sem["SET"], ld_small), (sem["V"], v_ones)]
                    if wval:
                        wts.append((wsem, wval))
                    tm = 0
                    for c in range(8):
                        tm = emit(
                            "t", "matmul", psx[:, 0:512],
                            attn[:, 2048 * c + 128 * t:2048 * c + 128 * t + 128],
                            wpT[:, 1024 * c + 512 * hh:1024 * c + 512 * hh + 512],
                            start=(c == 0), stop=False,
                            w=(wts if c == 0 else ()),
                            i=([("T", 1)] if c == 7 else ()))
                    tb = emit("t", "matmul", psx[:, 0:512],
                              onesr, bpb[:, 512 * hh:512 * hh + 512],
                              start=False, stop=True,
                              w=[(sem["T"], tm)], i=[("T", 1)])
                    vv = emit("v", "tensor_tensor",
                              stream_t[t][:, 512 * hh:512 * hh + 512],
                              psx[:, 0:512],
                              stream_t[t][:, 512 * hh:512 * hh + 512],
                              ALU.add,
                              w=[(sem["T"], tb),
                                 (sem["LSTR"], st["ld_stream"])],
                              i=[("V", 1)])
                    bg_rot[r] = (sem["V"], vv)
                    st[("res", t, hh)] = vv
                return f

            def mk_ln1_stats(t):
                def f():
                    v1 = emit("v", "reduce_sum", lnm16[:, t:t + 1],
                              stream_t[t], AX,
                              w=[(sem["V"], st[("res", t, 0)]),
                                 (sem["V"], st[("res", t, 1)])],
                              i=[("V", 1)])
                    wq_ = [(sem["V"], v1)]
                    if st.get(("dumpchain",), 0):
                        wq_.append((sem["V"], st[("dumpchain",)]))
                    vq = emit("v", "scalar_tensor_tensor", dump_dst(),
                              stream_t[t], 1.0, stream_t[t],
                              ALU.mult, ALU.mult,
                              w=wq_, i=[("V", 1)],
                              accum_out=lnq16[:, t:t + 1])
                    st[("stats", t)] = vq
                    st[("dumpchain",)] = vq
                return f

            def mk_ln_batch(which, c0, cnt=8):
                def f():
                    key = "stats" if which.startswith("ln1") else "stats2"
                    wts = [(sem["V"], st[(key, t)])
                           for t in range(c0, c0 + cnt)]
                    v2 = emit("v", "tensor_scalar", lnm16[:, c0:c0 + cnt],
                              lnm16[:, c0:c0 + cnt], 1.0 / D, None, ALU.mult,
                              w=wts, i=[("V", 1)])
                    v3 = emit("v", "tensor_scalar", lnq16[:, c0:c0 + cnt],
                              lnq16[:, c0:c0 + cnt], 1.0 / D, None, ALU.mult,
                              w=[(sem["V"], v2)], i=[("V", 1)])
                    v4 = emit("v", "tensor_tensor", lnv16[:, c0:c0 + cnt],
                              lnm16[:, c0:c0 + cnt], lnm16[:, c0:c0 + cnt],
                              ALU.mult, w=[(sem["V"], v3)], i=[("V", 1)])
                    v5 = emit("v", "tensor_tensor", lnv16[:, c0:c0 + cnt],
                              lnq16[:, c0:c0 + cnt], lnv16[:, c0:c0 + cnt],
                              ALU.subtract, w=[(sem["V"], v4)], i=[("V", 1)])
                    v5e = emit("v", "tensor_scalar", lnv16[:, c0:c0 + cnt],
                               lnv16[:, c0:c0 + cnt], 1e-5, None, ALU.add,
                               w=[(sem["V"], v5)], i=[("V", 1)])
                    sx = emit("s", "activation", lnr16[:, c0:c0 + cnt],
                              lnv16[:, c0:c0 + cnt], AF.Ln,
                              w=[(sem["V"], v5e)], i=[("S", 1)])
                    sy = emit("s", "activation", lnr16[:, c0:c0 + cnt],
                              lnr16[:, c0:c0 + cnt], AF.Exp, scale=-0.5,
                              w=[(sem["S"], sx)], i=[("S", 1)])
                    st[("rstd", which)] = sy
                return f

            def mk_ln1_apply(t):
                def f():
                    vv = emit("v", "tensor_scalar", stream_t[t], stream_t[t],
                              lnm16[:, t:t + 1], lnr16[:, t:t + 1],
                              ALU.subtract, ALU.mult,
                              w=[(sem["S"], st[("rstd", "ln1")])],
                              i=[("V", 1)])
                    st[("h1", t)] = vv
                return f

            def mk_tr(t, hh):
                def f():
                    r, psx = bg_bank()
                    wsem, wval = bg_rot[r]
                    wts = [(sem["V"], st[("h1", t)]), (sem["EYE"], ld_eye)]
                    if wval:
                        wts.append((wsem, wval))
                    tm = 0
                    for c4 in range(4):
                        c = 4 * hh + c4
                        tm = emit("t", "transpose",
                                  psx[:, 128 * c4:128 * c4 + 128],
                                  stream_t[t][:, 128 * c:128 * c + 128],
                                  eyeb,
                                  w=(wts if c4 == 0 else ()),
                                  i=([("T", 1)] if c4 == 3 else ()))
                    vv = emit("v", "tensor_copy",
                              ap(h1fm, 2048 * (4 * hh) + 128 * t,
                                 [[16384, 128], [2048, 4], [1, 128]]),
                              psx[:, 0:512],
                              w=[(sem["T"], tm)], i=[("V", 1)])
                    bg_rot[r] = (sem["V"], vv)
                    st[("h1fm", t, hh)] = vv
                return f

            def mk_router(t):
                def f():
                    r, psx = bg_bank()
                    wsem, wval = bg_rot[r]
                    wts = [(sem["V"], st[("h1fm", t, 0)]),
                           (sem["V"], st[("h1fm", t, 1)]),
                           (sem["SET"], ld_small)]
                    if wval:
                        wts.append((wsem, wval))
                    tm = 0
                    for c in range(8):
                        tm = emit("t", "matmul", psx[:, 0:8],
                                  h1fm[:, 2048 * c + 128 * t:2048 * c + 128 * t + 128],
                                  rwb[:, 8 * c:8 * c + 8],
                                  start=(c == 0), stop=(c == 7),
                                  w=(wts if c == 0 else ()),
                                  i=([("T", 1)] if c == 7 else ()))
                    vv = emit("v", "tensor_copy", logit[:, 8 * t:8 * t + 8],
                              psx[:, 0:8], w=[(sem["T"], tm)], i=[("V", 1)])
                    bg_rot[r] = (sem["V"], vv)
                    st[("logit", t)] = vv
                return f

            def gating():
                c0 = 8 * t0
                g0 = t0
                pr_ = ap(probs, c0, [[128, 128], [8, 8], [1, 8]])
                ga_ = ap(gates, c0, [[128, 128], [8, 8], [1, 8]])
                fl_p = ap(probs, c0, [[128, 128], [1, 64]])
                fl_g = ap(gates, c0, [[128, 128], [1, 64]])

                def b16(a):
                    return ap(a, g0, [[16, 128], [1, 8], [0, 8]])

                wts = [(sem["V"], st[("logit", t)]) for t in range(t0, t0 + 8)]
                sx = emit("s", "activation", fl_p,
                          ap(logit, c0, [[128, 128], [1, 64]]), AF.Exp,
                          w=wts, i=[("S", 1)])
                vv = emit("v", "reduce_sum", g_rd[:, g0:g0 + 8], pr_, AX,
                          w=[(sem["S"], sx)], i=[("V", 1)])
                vv = emit("v", "reciprocal", g_rd[:, g0:g0 + 8],
                          g_rd[:, g0:g0 + 8], w=[(sem["V"], vv)], i=[("V", 1)])
                vv = emit("v", "tensor_tensor", fl_p, fl_p, b16(g_rd),
                          ALU.mult, w=[(sem["V"], vv)], i=[("V", 1)])
                vv = emit("v", "reduce_max", g_v1[:, g0:g0 + 8], pr_, AX,
                          w=[(sem["V"], vv)], i=[("V", 1)])
                vv = emit("v", "tensor_tensor", fl_g, fl_p, b16(g_v1),
                          ALU.is_ge, w=[(sem["V"], vv)], i=[("V", 1)])
                vv = emit("v", "scalar_tensor_tensor", fl_g, fl_g, -100.0,
                          fl_p, ALU.mult, ALU.add,
                          w=[(sem["V"], vv)], i=[("V", 1)])
                vv = emit("v", "reduce_max", g_v2[:, g0:g0 + 8], ga_, AX,
                          w=[(sem["V"], vv)], i=[("V", 1)])
                vv = emit("v", "tensor_tensor", fl_g, fl_p, b16(g_v2),
                          ALU.is_ge, w=[(sem["V"], vv)], i=[("V", 1)])
                vv = emit("v", "tensor_tensor", g_rd[:, g0:g0 + 8],
                          g_v1[:, g0:g0 + 8], g_v2[:, g0:g0 + 8], ALU.add,
                          w=[(sem["V"], vv)], i=[("V", 1)])
                vv = emit("v", "tensor_scalar", g_rd[:, g0:g0 + 8],
                          g_rd[:, g0:g0 + 8], 1e-6, None, ALU.add,
                          w=[(sem["V"], vv)], i=[("V", 1)])
                vv = emit("v", "reciprocal", g_rd[:, g0:g0 + 8],
                          g_rd[:, g0:g0 + 8], w=[(sem["V"], vv)], i=[("V", 1)])
                vv = emit("v", "tensor_tensor", fl_g, fl_g, fl_p, ALU.mult,
                          w=[(sem["V"], vv)], i=[("V", 1)])
                vv = emit("v", "tensor_tensor", fl_g, fl_g, b16(g_rd),
                          ALU.mult, w=[(sem["V"], vv)], i=[("V", 1)])
                vv = emit("v", "tensor_tensor", g_t[:, g0:g0 + 8],
                          g_v1[:, g0:g0 + 8], g_v1[:, g0:g0 + 8], ALU.mult,
                          w=[(sem["V"], vv)], i=[("V", 1)])
                vv = emit("v", "tensor_tensor", g_v2[:, g0:g0 + 8],
                          g_v2[:, g0:g0 + 8], g_v2[:, g0:g0 + 8], ALU.mult,
                          w=[(sem["V"], vv)], i=[("V", 1)])
                vv = emit("v", "tensor_tensor", g_t[:, g0:g0 + 8],
                          g_t[:, g0:g0 + 8], g_v2[:, g0:g0 + 8], ALU.add,
                          w=[(sem["V"], vv)], i=[("V", 1)])
                v_g = emit("v", "tensor_tensor", g_t[:, g0:g0 + 8],
                           g_t[:, g0:g0 + 8], g_rd[:, g0:g0 + 8], ALU.mult,
                           w=[(sem["V"], vv)], i=[("V", 1)])
                st[("gates",)] = v_g

            def mk_ewload(e, hh, ci):
                def f():
                    wts = []
                    u = st.get(("ewuse", hh), 0)
                    if u:
                        wts.append((sem["T"], u))
                    if e == 0 and half == 0:
                        wts += [(sem["V"], st[("stats", t)])
                                for t in range(t0, t0 + 8)]
                    if e == 0 and half == 1:
                        wts.append((sem["T"], st[("projdone",)]))
                    if half == 0:
                        dst = ap(qall, 512 * hh + 2048 * 2 * ci,
                                 [[16384, 128], [2048, 2], [1, 512]])
                    else:
                        dst = ap(ewh2, 512 * hh + 1024 * 2 * ci,
                                 [[8192, 128], [1024, 2], [1, 512]])
                    st[("ewld", e, hh)] = emit(
                        "g", "dma_start", dst,
                        ap(d_ew, e * D * D + 512 * hh + 131072 * 2 * ci,
                           [[1024, 128], [131072, 2], [1, 512]]),
                        w=wts, i=[("EW%d" % hh, 16)])
                return f

            def mk_exp(e, t, hh):
                def f():
                    r, psx = bg_bank()
                    wsem, wval = bg_rot[r]
                    wts = [(sem["EW%d" % hh], st[("ewld", e, hh)]),
                           (sem["V"], st[("gates",)])]
                    if wval:
                        wts.append((wsem, wval))
                    tm = 0
                    for c in range(8):
                        tm = emit(
                            "t", "matmul", psx[:, 0:512],
                            h1fm[:, 2048 * c + 128 * t:2048 * c + 128 * t + 128],
                            ew_slice(c, hh),
                            start=(c == 0), stop=(c == 7),
                            w=(wts if c == 0 else ()),
                            i=([("T", 1)] if c == 7 else ()))
                    st[("ewuse", hh)] = tm
                    prev = st.get(("moe", t, hh), st[("h1", t)])
                    vv = emit("v", "scalar_tensor_tensor",
                              stream_t[t][:, 512 * hh:512 * hh + 512],
                              psx[:, 0:512],
                              gates[:, 8 * t + e:8 * t + e + 1],
                              stream_t[t][:, 512 * hh:512 * hh + 512],
                              ALU.mult, ALU.add,
                              w=[(sem["T"], tm), (sem["V"], prev)],
                              i=[("V", 1)])
                    bg_rot[r] = (sem["V"], vv)
                    st[("moe", t, hh)] = vv
                return f

            def mk_ln2_stats(t):
                def f():
                    vb_ = emit("v", "tensor_scalar", stream_t[t],
                               stream_t[t], g_t[:, t:t + 1], None, ALU.add,
                               w=[(sem["V"], st[("moe", t, 0)]),
                                  (sem["V"], st[("moe", t, 1)]),
                                  (sem["S"], st[("rstd", "ln1")])],
                               i=[("V", 1)])
                    v1 = emit("v", "reduce_sum", lnm16[:, t:t + 1],
                              stream_t[t], AX,
                              w=[(sem["V"], vb_)], i=[("V", 1)])
                    wq_ = [(sem["V"], v1),
                           (sem["V"], st[("dumpchain",)]),
                           (sem["EW1"], st[("ewld", NEXP - 1, 1)]),
                           (sem["T"], st[("ewuse", 0)]),
                           (sem["T"], st[("ewuse", 1)])]
                    vq = emit("v", "scalar_tensor_tensor", dump_dst(),
                              stream_t[t], 1.0, stream_t[t],
                              ALU.mult, ALU.mult,
                              w=wq_, i=[("V", 1)],
                              accum_out=lnq16[:, t:t + 1])
                    st[("stats2", t)] = vq
                    st[("dumpchain",)] = vq
                    if debug:
                        st[("r2d", t)] = emit(
                            "g", "dma_start",
                            dbg["res2"][:, 1024 * t:1024 * t + 1024],
                            stream_t[t], w=[(sem["V"], vq)], i=[("DBG", 16)])
                return f

            def mk_ln2_apply(t, rk):
                def f():
                    wts = [(sem["S"], st[("rstd", rk)])]
                    if debug:
                        wts.append((sem["DBG"], st[("r2d", t)]))
                    vv = emit("v", "tensor_scalar", stream_t[t], stream_t[t],
                              lnm16[:, t:t + 1], lnr16[:, t:t + 1],
                              ALU.subtract, ALU.mult,
                              w=wts,
                              i=[("V", 1)])
                    q = "s" if (half == 1 and t % 2 == 1) else "y"
                    emit(q, "dma_start",
                         ap(d_out, 1024 * 128 * t, [[1024, 128], [1, 1024]]),
                         stream_t[t],
                         w=[(sem["V"], vv)], i=[("LO", 16)])
                return f

            def projdone():
                st[("projdone",)] = ctr["T"]

            for t in range(t0, t0 + 8):
                th.append((1, mk_proj(t, 0)))
                th.append((1, mk_proj(t, 1)))
                th.append((2, mk_ln1_stats(t)))
            th.append((0, projdone))
            th.append((2, mk_ln_batch("ln1", t0)))
            for t in range(t0, t0 + 8):
                th.append((2, mk_ln1_apply(t)))
                th.append((1, mk_tr(t, 0)))
                th.append((1, mk_tr(t, 1)))
            for t in range(t0, t0 + 8):
                th.append((1, mk_router(t)))
            th.append((2, gating))
            for ci in range(4):
                th.append((0, mk_ewload(0, 0, ci)))
            for e in range(NEXP):
                for hh in range(2):
                    for t in range(t0, t0 + 8):
                        th.append((1, mk_exp(e, t, hh)))
                        if not (e == NEXP - 1 and hh == 1) and t - t0 >= 4:
                            ne, nh = (e, 1) if hh == 0 else (e + 1, 0)
                            th.append((0, mk_ewload(ne, nh, t - t0 - 4)))
            tail_th = []
            for g4 in range(2):
                b0 = t0 + 4 * g4
                for t in range(b0, b0 + 4):
                    th.append((2, mk_ln2_stats(t)))
                rk = "ln2" + ("a" if g4 == 0 else "b")
                th.append((2, mk_ln_batch(rk, b0, 4)))
                for t in range(b0, b0 + 4):
                    (tail_th if half == 0 else th).append(
                        (0, mk_ln2_apply(t, rk)))
            return th, st, tail_th

        seq = [(hp, p) for p in (0, 1) for hp in range(NHP)]
        kvq = iter(kv_groups(0, 0, 0))
        for g in kvq:
            g()
        st0 = None
        spb_user = {0: 0, 1: 0}
        for si, (hp, p) in enumerate(seq):
            kv_done = [False]
            if 1 <= si <= 4:
                load_wpT_chunk(si - 1)
            if si == 2:
                ld_stream1 = emit("y", "dma_start", str8,
                                  ap(d_xot, 0,
                                     [[1024, 128], [1024 * 128, 8], [1, 1024]]),
                                  w=[(sem["T"], guard[7])], i=[("LSTR", 16)])
            if p == 1 and hp == 0:
                th0, st0, tail0 = post_thunks(0)
                st0["ld_stream"] = ld_stream1
                work_q.extend(th0)
            nxt = seq[si + 1] if si + 1 < len(seq) else None
            kvq = iter(kv_groups(nxt[0], nxt[1], si + 1)) if nxt else iter(())
            buf = hp % 2
            kbase, vbase = 4096 * buf, 4160 * buf
            kr = kv_reload.get((hp, p))
            for slot in PASS_SLOTS[p]:
                npair = PAIRS[slot]
                for j in range(npair):
                    wts = []
                    if s_exp_prev:
                        wts.append((sem["S"], s_exp_prev))
                    else:
                        wts.append((sem["S"], qall_s[NHP - 1]))
                    tcn = (2 * j + 1) // 4
                    if p == 0 or tcn >= 4:
                        wts.append((sem["V"], kv_kevac[(hp, p)][tcn]))
                    if slot == PASS_SLOTS[p][0] and j == 0 and kr:
                        wts.append((sem[kr[0]], kr[1]))
                    t_sc = 0
                    for kk in range(2):
                        kgi = 2 * j + kk
                        for h in range(2):
                            r0, r1 = (0, 64) if h == 0 else (64, 128)
                            t_sc = emit(
                                "t", "matmul",
                                psS[:, 1024 * kk + 512 * h:1024 * kk + 512 * h + 512],
                                kb[r0:r1, kbase + 128 * kgi:kbase + 128 * kgi + 128],
                                qall[r0:r1, 2048 * hp + 512 * slot:2048 * hp + 512 * slot + 512],
                                start=True, stop=True,
                                w=(wts if (kk == 0 and h == 0) else ()),
                                i=([("T", 1)] if (kk == 1 and h == 1) else ()))
                    if len(av_queue) >= 2:
                        pop_av()
                    if not (slot == PASS_SLOTS[p][0] and j == 0):
                        for _ in range(2):
                            ng = next(kvq, None)
                            if ng is not None:
                                ng()
                            elif not kv_done[0]:
                                kv_done[0] = True
                                if si + 2 < len(seq):
                                    load_wq_usage(8 + si + 2,
                                                  kv_tlast[seq[si]])
                                    nhp, npp = seq[si + 2]
                                    for tc in ((0, 1) if npp == 0 else (4, 5)):
                                        if (npp, nhp, tc) not in xk_load:
                                            load_xk(npp, nhp, tc)
                    v_cool[0] = max(0, v_cool[0] - 1)
                    heavy_done = 0
                    pops = 0
                    if slot == PASS_SLOTS[p][1] and j >= npair - 2:
                        heavy_done = 1
                    if epi_q:
                        epi_q.pop(0)()
                        epi_pops[0] += 1
                    while (work_q and pops < 3
                           and epi_pops[0] >= P0_EPI):
                        tag = work_q[0][0]
                        if tag == 1 and heavy_done:
                            break
                        if tag == 2 and (heavy_done or v_cool[0]):
                            break
                        fn = work_q.pop(0)[1]
                        fn()
                        pops += 1
                        if tag:
                            heavy_done = 1
                        if tag == 2:
                            v_cool[0] = 3
                    par = gpair[0] % 2
                    gpair[0] += 1
                    wexp = [(sem["T"], t_sc)]
                    s_exp_prev = emit("s", "activation", t_exp,
                                      psS[:, 0:2048], AF.Exp,
                                      w=wexp, i=[("S", 1)], scale=0.125)
                    wln = [(sem["S"], s_exp_prev)]
                    if spb_user[0]:
                        wln.append((sem["V"], spb_user[0]))
                    s_ln_prev = emit("s", "activation", spb, t_exp, AF.Ln,
                                     w=wln, i=[("S", 1)], bias=1.0)
                    v_sq_raw = emit("v", "tensor_tensor",
                                    pbb[:, 2048 * par:2048 * par + 2048],
                                    spb, spb, ALU.mult,
                                    w=[(sem["S"], s_ln_prev),
                                       (sem["T"], t_av_prev)], i=[("V", 1)])
                    spb_user[0] = v_sq_raw
                    v_sq_prev = v_sq_raw
                    if j >= npair - 4:
                        vw = [(sem["V"], v_sq_raw)]
                        if not mask_waited[0]:
                            vw.append((sem["LSC"], ld_mask))
                            mask_waited[0] = True
                        m0 = 2 * (j - (npair - 4))
                        v_sq_prev = emit(
                            "v", "tensor_tensor",
                            ap(pbb, 2048 * par,
                               [[4096, 128], [1024, 2], [512, 2], [1, 512]]),
                            ap(pbb, 2048 * par,
                               [[4096, 128], [1024, 2], [512, 2], [1, 512]]),
                            ap(mkb, 512 * m0,
                               [[4096, 128], [512, 2], [0, 2], [1, 512]]),
                            ALU.mult, w=vw, i=[("V", 1)])
                    av_queue.append((slot, j, v_sq_prev, par, hp, p,
                                     j == npair - 1))
            for g in kvq:
                g()
            if not kv_done[0] and si + 2 < len(seq):
                load_wq_usage(8 + si + 2, kv_tlast[seq[si + 1]])
                nhp, npp = seq[si + 2]
                for tc in ((0, 1) if npp == 0 else (4, 5)):
                    if (npp, nhp, tc) not in xk_load:
                        load_xk(npp, nhp, tc)
        while av_queue:
            pop_av()
        while epi_q:
            epi_q.pop(0)()

        if debug:
            emit("g", "dma_start", dbg["attn"], attn,
                 w=[(sem["V"], ctr["V"]), (sem["T"], ctr["T"])],
                 i=[("DBG", 16)])
            emit("g", "dma_start", dbg["kb"], kb, i=[("DBG", 16)])
            emit("g", "dma_start", dbg["vb"], vb, i=[("DBG", 16)])
            emit("g", "dma_start", dbg["qall"], qall, i=[("DBG", 16)])

        th1, st1, tail1 = post_thunks(1)
        w_s2 = [(sem["T"], ctr["T"]), (sem["V"], ctr["V"]),
                (sem["S"], ctr["S"])]
        emit("y", "dma_start", str4a,
             ap(d_xot, 8 * 128 * 1024,
                [[1024, 128], [1024 * 128, 4], [1, 1024]]),
             w=w_s2, i=[("LSTR", 16)])
        emit("y", "dma_start", str4b,
             ap(d_xot, 12 * 128 * 1024,
                [[1024, 128], [1024 * 128, 4], [1, 1024]]),
             w=w_s2, i=[("LSTR", 16)])
        st1["ld_stream"] = ctr["LSTR"]
        q1 = list(th1)
        while work_q or tail0 or q1:
            if work_q:
                work_q.pop(0)[1]()
            elif tail0:
                tail0.pop(0)[1]()
            if q1:
                q1.pop(0)[1]()
        if debug:
            emit("g", "dma_start", dbg["h1fm"], h1fm,
                 w=[(sem["V"], ctr["V"])], i=[("DBG", 16)])
            emit("g", "dma_start", dbg["gates"], gates, i=[("DBG", 16)])
            emit("g", "dma_start", dbg["bias16"], g_t, i=[("DBG", 16)])
        emit("y", "wait_ge", sem["LO"], ctr["LO"])

        block = En(nc.Block())

        def runner(key):
            def run(eng):
                for waits, method, args, kw, incs in recs[key]:
                    for s, val in waits:
                        if val:
                            eng.wait_ge(s, val)
                    if method == "wait_ge":
                        eng.wait_ge(*args)
                        continue
                    instr = getattr(eng, method)(*args, **kw)
                    for nm, n in incs:
                        instr.then_inc(sem[nm], n)
            return run

        block.tensor(runner("t"))
        block.scalar(runner("s"))
        block.vector(runner("v"))
        block.gpsimd(runner("g"))
        block.sync(runner("y"))

    return nc



def _role_masks(role):
    q = np.arange(QS)[None, :]
    k = np.arange(128)[:, None]
    tiles = []
    for j in range(8):
        if role == 0:
            m = (np.ones((128, QS), np.float32) if j < 4
                 else (q >= 128 * (j - 4) + k).astype(np.float32))
        else:
            m = ((q >= 128 * j + k).astype(np.float32) if j < 4
                 else np.zeros((128, QS), np.float32))
        tiles.append(m)
    return np.ascontiguousarray(
        np.concatenate(tiles, axis=1).astype(ml_dtypes.bfloat16))


_NC_CACHE = {}


def kernel(L0_attn_x, L0_attn_w_qkv, L0_attn_w_proj, L0_attn_b_proj,
           L0_ffn_router_w, L0_ffn_expert_weights):
    x = np.asarray(L0_attn_x, np.float32)
    w_qkv = np.asarray(L0_attn_w_qkv, np.float32)
    w_proj = np.asarray(L0_attn_w_proj, np.float32)
    b_proj = np.asarray(L0_attn_b_proj, np.float32)
    router_w = np.asarray(L0_ffn_router_w, np.float32)
    ew = np.ascontiguousarray(np.asarray(L0_ffn_expert_weights, np.float32))

    wq8 = np.empty((NHP, D, 384), np.float32)
    for hp in range(NHP):
        rows = np.r_[128 * hp:128 * hp + 128,
                     1024 + 128 * hp:1024 + 128 * hp + 128,
                     2048 + 128 * hp:2048 + 128 * hp + 128]
        wq8[hp] = w_qkv[rows, :].T
    wpT = np.ascontiguousarray(w_proj.T)
    rwT = np.ascontiguousarray(router_w.T)
    bp = np.ascontiguousarray(b_proj[None, :])
    eye = np.eye(128, dtype=np.float32)
    masks = {r: _role_masks(r) for r in (0, 1)}

    in_maps = []
    blocks = {}
    for core in range(8):
        b, role = core // 2, core % 2
        gs = [e - 512 for e in ROLE_EXT[role]]
        blocks[core] = gs
        own = np.concatenate([np.arange(g, g + 512) for g in gs])
        xb = x[b]
        in_maps.append({
            "xkf": np.ascontiguousarray(xb.T),
            "xqf": np.ascontiguousarray(xb[own].T),
            "xot": np.ascontiguousarray(xb[own]),
            "wq8": wq8, "wpT": wpT, "bp": bp, "rwT": rwT, "ew": ew,
            "mk": masks[role], "eyein": eye,
        })

    if "nc" not in _NC_CACHE:
        _NC_CACHE["nc"] = build_program()
    nc = _NC_CACHE["nc"]
    res = run_bass_kernel_spmd(nc, in_maps, core_ids=list(range(8)))
    out = np.empty((B, T, D), np.float32)
    for core in range(8):
        co = np.asarray(res.results[core]["out"], np.float32)
        b = core // 2
        for s, g in enumerate(blocks[core]):
            out[b, g:g + 512] = co[512 * s:512 * (s + 1)]
    return out


# version 29
# speedup vs baseline: 1.0585x; 1.0177x over previous
import contextlib
import numpy as np
import ml_dtypes
import concourse.bass as bass
import concourse.mybir as mybir
from concourse.bass_utils import run_bass_kernel_spmd

F32 = mybir.dt.float32
BF16 = mybir.dt.bfloat16
FP8 = mybir.dt.float8e4
AF = mybir.ActivationFunctionType
ALU = mybir.AluOpType
AX = mybir.AxisListType.X

B, T, D = 4, 4096, 1024
NHP = 8
NSLOT, QS = 4, 512
OWN = NSLOT * QS
NTT = OWN // 128
ENV = [1024, 2048, 3072, 4096]
KG = [e // 128 for e in ENV]
PAIRS = [k // 2 for k in KG]
NEXP = 8
ROLE_EXT = {0: [1024, 2048, 3072, 4096], 1: [512, 1536, 2560, 3584]}
PASS_SLOTS = {0: (0, 1), 1: (2, 3)}

SB0 = 17408
OFF_XKB = SB0 + 0
OFF_KB = SB0 + 16384
OFF_VB = SB0 + 32768
OFF_STR = SB0 + 49408
OFF_ATTN = SB0 + 82176
OFF_QALL = SB0 + 114944
OFF_WPT = SB0 + 147712
OFF_T = SB0 + 164096
OFF_SP = SB0 + 168192
OFF_P = SB0 + 172288
OFF_RS = SB0 + 180480
OFF_WQ = SB0 + 184576
OFF_MASK = SB0 + 192768
OFF_RS1 = SB0 + 200960
OFF_EYE = SB0 + 205056
OFF_RWB = SB0 + 205568
OFF_GATES = SB0 + 206080
OFF_LOG = SB0 + 206592
OFF_PROBS = SB0 + 207104
OFF_G16 = SB0 + 207616
OFF_LN16 = SB0 + 207872
OFF_BPB = SB0 + 208128
OFF_ONES = SB0 + 210176
OFF_BC = SB0 + 210432
assert OFF_BC + 1024 <= 229344

STREAM_OFF = {}
for _t in range(8):
    STREAM_OFF[_t] = OFF_STR + 4096 * _t
for _t in range(8, 12):
    STREAM_OFF[_t] = OFF_XKB + 4096 * (_t - 8)
for _t in range(12, 16):
    STREAM_OFF[_t] = OFF_T + 4096 * (_t - 12)

DEBUG = False


def build_program(debug=DEBUG):
    nc = bass.Bass("TRN2", target_bir_lowering=False, debug=False)

    d_xkf = nc.dram_tensor("xkf", [D, T], F32, kind="ExternalInput").ap()
    d_xqf = nc.dram_tensor("xqf", [D, OWN], F32, kind="ExternalInput").ap()
    d_xot = nc.dram_tensor("xot", [OWN, D], F32, kind="ExternalInput").ap()
    d_wq8 = nc.dram_tensor("wq8", [NHP, D, 384], F32, kind="ExternalInput").ap()
    d_wpT = nc.dram_tensor("wpT", [D, D], F32, kind="ExternalInput").ap()
    d_bp = nc.dram_tensor("bp", [1, D], F32, kind="ExternalInput").ap()
    d_rwT = nc.dram_tensor("rwT", [D, NEXP], F32, kind="ExternalInput").ap()
    d_ew = nc.dram_tensor("ew", [NEXP, D, D], F32, kind="ExternalInput").ap()
    d_mk = nc.dram_tensor("mk", [128, 8 * QS], BF16, kind="ExternalInput").ap()
    d_eye = nc.dram_tensor("eyein", [128, 128], F32, kind="ExternalInput").ap()
    d_out = nc.dram_tensor("out", [OWN, D], F32, kind="ExternalOutput").ap()
    d_scr = nc.dram_tensor("scr", [2 * 8, QS], F32, kind="Internal").ap()
    d_kvs = nc.dram_tensor("kvs", [NHP, 128, 4128], BF16, kind="Internal").ap()
    dbg = {}
    if debug:
        for nm, shp in [("qall", [128, 16384]), ("kb", [128, 8192]),
                        ("vb", [128, 8320]), ("attn", [128, 16384]),
                        ("h1fm", [128, 16384]), ("gates", [128, 128]),
                        ("bias16", [128, 16]), ("res2", [128, 16384])]:
            dbg[nm] = nc.dram_tensor("dbg_" + nm, shp, F32,
                                     kind="ExternalOutput").ap()

    A = nc.alloc_sbuf_tensor_at
    xkb = [A("xkb%d" % i, [128, 4096], BF16, offset=OFF_XKB + 8192 * i).ap()
           for i in range(2)]
    kb = A("kb", [128, 8192], BF16, offset=OFF_KB).ap()
    vb = A("vb", [128, 8320], BF16, offset=OFF_VB).ap()
    str8 = A("str8", [128, 8192], F32, offset=OFF_STR).ap()
    str4a = A("str4a", [128, 4096], F32, offset=OFF_XKB).ap()
    str4b = A("str4b", [128, 4096], F32, offset=OFF_T).ap()
    stream_t = [A("stream%d" % t, [128, 1024], F32, offset=STREAM_OFF[t]).ap()
                for t in range(16)]
    attn = A("attn", [128, 16384], BF16, offset=OFF_ATTN).ap()
    h1fm = A("h1fm", [128, 16384], BF16, offset=OFF_ATTN).ap()
    qall = A("qall", [128, 16384], BF16, offset=OFF_QALL).ap()
    wpT = A("wpT", [128, 8192], BF16, offset=OFF_WPT).ap()
    ewh2 = A("ewh2", [128, 8192], BF16, offset=OFF_WPT).ap()
    xq0 = A("xq0", [128, 8192], BF16, offset=OFF_T).ap()
    xq1 = A("xq1", [128, 8192], BF16, offset=OFF_ATTN).ap()
    t_exp = A("t_exp", [128, 2048], BF16, offset=OFF_T).ap()
    spb = A("spb", [128, 2048], BF16, offset=OFF_SP).ap()
    pbb = A("pbb", [128, 4096], BF16, offset=OFF_P).ap()
    bcb = A("bcb", [128, 512], BF16, offset=OFF_BC).ap()
    rsb2 = [A("rsb0", [128, 1024], F32, offset=OFF_RS).ap(),
            A("rsb1", [128, 1024], F32, offset=OFF_RS1).ap()]
    wqK = [A("wqK%d" % i, [128, 2048], BF16, offset=OFF_WQ + 4096 * i).ap()
           for i in range(2)]
    wqA = [A("wqA%d" % i, [128, 3072], BF16, offset=OFF_STR + 6144 * i).ap()
           for i in range(2)]
    mkb = A("mkb", [128, 4096], BF16, offset=OFF_MASK).ap()
    eyeb = A("eyeb", [128, 128], F32, offset=OFF_EYE).ap()
    rwb = A("rwb", [128, 64], BF16, offset=OFF_RWB).ap()
    gates = A("gates", [128, 128], F32, offset=OFF_GATES).ap()
    logit = A("logit", [128, 128], F32, offset=OFF_LOG).ap()
    probs = A("probs", [128, 128], F32, offset=OFF_PROBS).ap()
    g_v1 = A("g_v1", [128, 16], F32, offset=OFF_G16).ap()
    g_v2 = A("g_v2", [128, 16], F32, offset=OFF_G16 + 64).ap()
    g_rd = A("g_rd", [128, 16], F32, offset=OFF_G16 + 128).ap()
    g_t = A("g_t", [128, 16], F32, offset=OFF_G16 + 192).ap()
    lnm16 = A("lnm16", [128, 16], F32, offset=OFF_LN16).ap()
    lnq16 = A("lnq16", [128, 16], F32, offset=OFF_LN16 + 64).ap()
    lnv16 = A("lnv16", [128, 16], F32, offset=OFF_LN16 + 128).ap()
    lnr16 = A("lnr16", [128, 16], F32, offset=OFF_LN16 + 192).ap()
    bpb = A("bpb", [1, 1024], BF16, offset=OFF_BPB).ap()
    onesr = A("onesr", [1, 128], BF16, offset=OFF_ONES).ap()

    def ap(base, off, dims):
        return bass.AP(base.tensor, off, dims)

    recs = {k: [] for k in "tsvgy"}
    SEMN = ("T", "S", "V", "LO", "XS", "XS1", "XS2", "XS3",
            "WQ0", "WQ1", "XB0", "XB1",
            "KS0", "KS1", "KL0", "KL1", "SCR0", "SCR1", "BC", "EW0", "EW1",
            "WPT", "SET", "LSC", "EYE", "LSTR", "DBG")
    ctr = {k: 0 for k in SEMN}

    with contextlib.ExitStack() as ctx:
        En = ctx.enter_context
        psS = En(nc.psum_tensor("psS", [128, 2048], F32)).ap()
        psA = En(nc.psum_tensor("psA", [128, 512], F32)).ap()
        psB = En(nc.psum_tensor("psB", [128, 512], F32)).ap()
        psK = En(nc.psum_tensor("psK", [128, 512], F32)).ap()
        psV = En(nc.psum_tensor("psV", [128, 512], F32)).ap()
        sem = {n: En(nc.semaphore("s" + n)) for n in SEMN}

        def emit(e, method, *args, w=(), i=(), **kw):
            out = []
            for nm, n in i:
                ctr[nm] += n
                out.append(ctr[nm])
            recs[e].append((tuple(w), method, args, kw, tuple(i)))
            return out[0] if len(out) == 1 else (out or 0)

        ld_mask = emit("y", "dma_start", mkb, d_mk, i=[("LSC", 16)])
        ld_eye = emit("y", "dma_start", eyeb, d_eye, i=[("EYE", 16)])
        ld_stream1 = 0
        ld_xqp = []

        def load_xq_parts():
            for part in range(4):
                xdst = xq0 if part < 2 else xq1
                sn = "XS" if part == 0 else "XS%d" % part
                ld_xqp.append((sn, emit(
                    "g", "dma_start",
                    ap(xdst, 4096 * (part % 2),
                       [[8192, 128], [2048, 2], [1, 2048]]),
                    ap(d_xqf, 256 * OWN * part,
                       [[2048, 128], [2048 * 128, 2], [1, 2048]]),
                    i=[(sn, 16)])))
            emit("g", "dma_start", bpb, d_bp, i=[("SET", 16)])
            return emit("g", "dma_start", rwb,
                        ap(d_rwT, 0, [[8, 128], [8 * 128, 8], [1, 8]]),
                        i=[("SET", 16)])
        ld_wpT = 4 * 16

        def load_wpT_chunk(c):
            emit("g", "dma_start",
                 ap(wpT, 2048 * c, [[8192, 128], [1024, 2], [1, 1024]]),
                 ap(d_wpT, 131072 * 2 * c, [[1024, 128], [131072, 2], [1, 1024]]),
                 i=[("WPT", 16)])
        emit("v", "memset",
             ap(vb, 64, [[8320, 128], [4160, 2], [130, 32], [65, 2]]), 1.0,
             i=[("V", 1)])
        emit("v", "memset", onesr, 1.0, i=[("V", 1)])
        emit("v", "memset", rsb2[0], 1.0, i=[("V", 1)])
        v_ones = emit("v", "memset", rsb2[1], 1.0, i=[("V", 1)])

        ld_wq = {}

        def load_wq_usage(k, guard_t):
            hp = k % 8
            qn = "WQ%d" % (k % 2)
            wts = [(sem["T"], guard_t)] if guard_t else []
            if k < 8:
                dst = wqA[k % 2]
                src = ap(d_wq8, hp * D * 384,
                         [[384, 128], [384 * 128, 8], [1, 384]])
            else:
                dst = wqK[k % 2]
                src = ap(d_wq8, hp * D * 384 + 128,
                         [[384, 128], [384 * 128, 8], [1, 256]])
            ld_wq[k] = (qn, emit("g", "dma_start", dst, src,
                                 w=wts, i=[(qn, 16)]))

        load_wq_usage(0, 0)
        load_wq_usage(1, 0)
        ld_small = load_xq_parts()

        xk_load = {}
        xkb_guard = {0: 0, 1: 0}

        def load_xk(p, hp, tc):
            bi = tc % 2
            qn = "XB%d" % bi
            wts = [(sem["T"], xkb_guard[bi])] if xkb_guard[bi] else []
            xk_load[(p, hp, tc)] = (qn, emit(
                "g", "dma_start", xkb[bi],
                ap(d_xkf, 512 * tc, [[4096, 128], [4096 * 128, 8], [1, 512]]),
                w=wts, i=[(qn, 16)]))

        load_xk(0, 0, 0)
        load_xk(0, 0, 1)

        evac_rot = {r: (sem["S"], 0) for r in range(4)}
        qall_s = {}
        guard = {}
        for hp in range(NHP):
            for qt in range(NSLOT):
                gi = hp * NSLOT + qt
                rot = gi % 4
                ps = psS[:, 512 * rot:512 * rot + 512]
                wsem, wval = evac_rot[rot]
                qn, qv = ld_wq[hp]
                wts = [(sem[qn], qv),
                       (sem[ld_xqp[0][0]], ld_xqp[0][1])]
                if wval:
                    wts.append((wsem, wval))
                lq = 0
                for c in range(8):
                    xsrc = xq0 if c < 4 else xq1
                    cc = c if c < 4 else c - 4
                    if c == 0:
                        wc = wts
                    elif c in (2, 4, 6):
                        wc = [(sem[ld_xqp[c // 2][0]], ld_xqp[c // 2][1])]
                    else:
                        wc = ()
                    lq = emit(
                        "t", "matmul", ps,
                        wqA[hp % 2][:, 384 * c:384 * c + 128],
                        xsrc[:, 2048 * cc + 512 * qt:2048 * cc + 512 * qt + 512],
                        start=(c == 0), stop=(c == 7),
                        w=wc,
                        i=([("T", 1)] if c == 7 else ()))
                col = 2048 * hp + 512 * qt
                sv = emit("s", "copy", qall[:, col:col + 512],
                          ps, w=[(sem["T"], lq)], i=[("S", 1)])
                evac_rot[rot] = (sem["S"], sv)
                qall_s[hp] = sv
                guard[hp] = lq
            if hp + 2 < NHP:
                load_wq_usage(hp + 2, guard[hp])
        load_wq_usage(8, guard[6])
        load_wq_usage(9, guard[7])

        bg_rot = {0: (sem["S"], 0), 1: (sem["S"], 0)}
        bg_i = [0]

        def bg_bank():
            r = bg_i[0] % 2
            bg_i[0] += 1
            return r, (psK if r == 0 else psV)

        kv_tlast = {}
        kv_kevac = {}
        kv_vevac = {}
        kv_reload = {}
        ks_val = {0: 0, 1: 0}
        t_av_box = [0]

        def kv_groups(hp, p, si):
            buf = hp % 2
            kbase, vbase = 4096 * buf, 4160 * buf
            use = 8 + si
            tcs = (0, 1, 2, 3) if p == 0 else (4, 5, 6, 7)
            kv_kevac[(hp, p)] = {}
            kv_vevac[(hp, p)] = {}
            if p == 1:
                kn, sn = "KL%d" % buf, "KS%d" % buf
                w_r = [(sem["T"], t_av_box[0]), (sem[sn], ks_val[buf])]
                emit("y", "dma_start", kb[:, kbase:kbase + 2048],
                     ap(d_kvs, hp * 128 * 4128, [[4128, 128], [1, 2048]]),
                     w=w_r, i=[(kn, 16)])
                kl = emit("y", "dma_start", vb[:, vbase:vbase + 2080],
                          ap(d_kvs, hp * 128 * 4128 + 2048,
                             [[4128, 128], [1, 2080]]),
                          w=w_r, i=[(kn, 16)])
                kv_reload[(hp, p)] = (kn, kl)
            for tc in tcs[:2]:
                if (p, hp, tc) not in xk_load:
                    load_xk(p, hp, tc)
            for tc in tcs:
                bi = tc % 2
                qx, vx = xk_load[(p, hp, tc)]
                qn, qv = ld_wq[use]

                def gK(tc=tc, bi=bi, qx=qx, vx=vx, qn=qn, qv=qv,
                       kbase=kbase, buf=buf, p=p, hp=hp):
                    r, psx = bg_bank()
                    wsem, wval = bg_rot[r]
                    wts = [(sem[qn], qv), (sem[qx], vx)]
                    if wval:
                        wts.append((wsem, wval))
                    tm = 0
                    for c in range(8):
                        tm = emit("t", "matmul", psx[:, 0:512],
                                  wqK[buf][:, 256 * c:256 * c + 128],
                                  xkb[bi][:, 512 * c:512 * c + 512],
                                  start=(c == 0), stop=(c == 7),
                                  w=(wts if c == 0 else ()),
                                  i=([("T", 1)] if c == 7 else ()))
                    wev = [(sem["T"], tm)]
                    if p == 0 and ks_val[buf]:
                        wev.append((sem["KS%d" % buf], ks_val[buf]))
                    vv = emit("v", "tensor_copy",
                              kb[:, kbase + 512 * tc:kbase + 512 * tc + 512],
                              psx[:, 0:512], w=wev, i=[("V", 1)])
                    bg_rot[r] = (sem["V"], vv)
                    kv_tlast[(hp, p)] = tm
                    kv_kevac[(hp, p)][tc] = vv
                yield gK
                for i4 in range(4):
                    def gV(tc=tc, bi=bi, i4=i4, qx=qx, vx=vx, qn=qn, qv=qv,
                           kbase=kbase, vbase=vbase, buf=buf, p=p, hp=hp,
                           tcs=tcs):
                        tv = 4 * tc + i4
                        r, psx = bg_bank()
                        wsem, wval = bg_rot[r]
                        wts = [(sem[qn], qv), (sem[qx], vx)]
                        if wval:
                            wts.append((wsem, wval))
                        tm = 0
                        for c in range(8):
                            tm = emit(
                                "t", "matmul", psx[:, 0:128],
                                xkb[bi][:, 512 * c + 128 * i4:512 * c + 128 * i4 + 128],
                                wqK[buf][:, 256 * c + 128:256 * c + 256],
                                start=(c == 0), stop=(c == 7),
                                w=(wts if c == 0 else ()),
                                i=([("T", 1)] if c == 7 else ()))
                        wev = [(sem["T"], tm)]
                        if p == 0 and ks_val[buf]:
                            wev.append((sem["KS%d" % buf], ks_val[buf]))
                        vv = emit("v", "tensor_copy",
                                  ap(vb, vbase + 130 * tv,
                                     [[8320, 128], [65, 2], [1, 64]]),
                                  ap(psK if r == 0 else psV, 0,
                                     [[512, 128], [64, 2], [1, 64]]),
                                  w=wev, i=[("V", 1)])
                        bg_rot[r] = (sem["V"], vv)
                        kv_tlast[(hp, p)] = tm
                        kv_vevac[(hp, p)][tv] = vv
                        if i4 == 3:
                            xkb_guard[bi] = tm
                            if tc + 2 in tcs:
                                load_xk(p, hp, tc + 2)
                            if p == 0 and tc == tcs[-1]:
                                sn = "KS%d" % buf
                                wsp = [(sem["V"], vv)]
                                emit("y", "dma_start",
                                     ap(d_kvs, hp * 128 * 4128,
                                        [[4128, 128], [1, 2048]]),
                                     kb[:, kbase:kbase + 2048],
                                     w=wsp, i=[(sn, 16)])
                                ks_val[buf] = emit(
                                    "y", "dma_start",
                                    ap(d_kvs, hp * 128 * 4128 + 2048,
                                       [[4128, 128], [1, 2080]]),
                                    vb[:, vbase:vbase + 2080],
                                    w=wsp, i=[(sn, 16)])
                    yield gV

        s_exp_prev = 0
        s_ln_prev = 0
        v_sq_prev = 0
        t_av_prev = ctr["T"]
        av_evac_prev = 0
        v_apply_last = 0
        lg_scr = {0: 0, 1: 0}
        rs_copy_last = 0
        av_queue = []
        epi_q = []
        work_q = []
        hp_start_v_cur = [0]
        gpair = [0]
        mask_waited = [False]
        epi_pops = [0]
        v_cool = [0]
        P0_EPI = NHP * 9

        def emit_av(slot, j, hp_, p_, v_sq, par):
            nonlocal t_av_prev
            nkg = KG[slot]
            vbase = 4160 * (hp_ % 2)
            pb0 = 2048 * par
            for kk in range(2):
                kgi = 2 * j + kk
                for h in range(2):
                    ps = psA if h == 0 else psB
                    wts = []
                    if kk == 0 and h == 0:
                        wts.append((sem["V"], v_sq))
                        tvn = 2 * j + 1
                        if p_ == 0 or tvn >= 16:
                            wts.append((sem["V"], kv_vevac[(hp_, p_)][tvn]))
                        if kgi == 0 and av_evac_prev:
                            wts.append((sem["V"], av_evac_prev))
                    t_av_prev = emit(
                        "t", "matmul", ps[0:65, :],
                        vb[:, vbase + 130 * kgi + 65 * h:vbase + 130 * kgi + 65 * h + 65],
                        pbb[:, pb0 + 1024 * kk + 512 * h:pb0 + 1024 * kk + 512 * h + 512],
                        start=(kgi == 0), stop=(kgi == nkg - 1),
                        w=wts, i=([("T", 1)] if (kk == 1 and h == 1) else ()))
            t_av_box[0] = t_av_prev

        def pop_av():
            nonlocal av_evac_prev, rs_copy_last
            slot_, j_, vsq_, par_, hp_, p_, last_ = av_queue.pop(0)
            emit_av(slot_, j_, hp_, p_, vsq_, par_)
            if last_:
                p_ = hp_ % 2
                rsb = rsb2[p_]
                acol = 2048 * hp_ + 512 * slot_
                emit("v", "tensor_copy", attn[0:64, acol:acol + 512],
                     psA[0:64, :], w=[(sem["T"], t_av_prev)], i=[("V", 1)])
                emit("v", "tensor_copy", attn[64:128, acol:acol + 512],
                     psB[0:64, :], i=[("V", 1)])
                kA, kB = 2 * slot_, 2 * slot_ + 1
                rs_w = [(sem["SCR%d" % p_], lg_scr[p_])] if (
                    slot_ % 2 == 0 and lg_scr[p_]) else []
                emit("v", "tensor_copy",
                     rsb[32 * (kA % 4):32 * (kA % 4) + 1,
                         512 * (kA // 4):512 * (kA // 4) + 512],
                     psA[64:65, :], w=rs_w, i=[("V", 1)])
                av_evac_prev = emit("v", "tensor_copy",
                                    rsb[32 * (kB % 4):32 * (kB % 4) + 1,
                                        512 * (kB // 4):512 * (kB // 4) + 512],
                                    psB[64:65, :], i=[("V", 1)])
                rs_copy_last = av_evac_prev
                if slot_ % 2 == 1:
                    queue_epilogue(hp_, slot_ // 2)

        class EpiState:
            vv = 0
            v_recip = 0

        def queue_epilogue(hp_, pa):
            es = EpiState()
            p_ = hp_ % 2
            rsb = rsb2[p_]
            scr_n = "SCR%d" % p_
            ch = 512 * pa

            def t_eps():
                es.vv = emit("v", "tensor_scalar", rsb[:, ch:ch + 512],
                             rsb[:, ch:ch + 512], 1e-9, None,
                             ALU.add, w=[(sem["V"], rs_copy_last)],
                             i=[("V", 1)])

            def mk_recip(k4):
                def t_recip():
                    es.v_recip = emit("v", "reciprocal",
                                      rsb[:, 256 * k4:256 * k4 + 256],
                                      rsb[:, 256 * k4:256 * k4 + 256],
                                      w=[(sem["V"], es.vv)], i=[("V", 1)])
                return t_recip

            def mk_scr(k):
                def t_scr():
                    scr_w = [(sem["V"], es.v_recip)]
                    if v_apply_last:
                        scr_w.append((sem["V"], v_apply_last))
                    lg_scr[p_] = emit(
                        "g", "dma_start",
                        ap(d_scr, (p_ * 8 + k) * QS, [[512, 1], [1, 512]]),
                        rsb[32 * (k % 4):32 * (k % 4) + 1,
                            512 * (k // 4):512 * (k // 4) + 512],
                        w=scr_w, i=[(scr_n, 16)])
                return t_scr

            def mk_bc(s2):
                def t_bc():
                    nonlocal v_apply_last
                    lg2 = 0
                    for h in range(2):
                        bc_w = [(sem[scr_n], lg_scr[p_])]
                        if v_apply_last:
                            bc_w.append((sem["V"], v_apply_last))
                        lg2 = emit("g", "dma_start",
                                   bcb[64 * h:64 * h + 64, :],
                                   ap(d_scr, (p_ * 8 + 2 * s2 + h) * QS,
                                      [[0, 64], [1, 512]]),
                                   w=(bc_w if h == 0 else ()),
                                   i=[("BC", 16)])
                    acol = 2048 * hp_ + 512 * s2
                    v_apply_last = emit(
                        "v", "tensor_tensor", attn[:, acol:acol + 512],
                        attn[:, acol:acol + 512], bcb[:, :], ALU.mult,
                        w=[(sem["BC"], lg2)], i=[("V", 1)])
                return t_bc

            epi_q.append(t_eps)
            for k4 in (2 * pa, 2 * pa + 1):
                epi_q.append(mk_recip(k4))
            for k in range(4 * pa, 4 * pa + 4):
                epi_q.append(mk_scr(k))
            for s2 in (2 * pa, 2 * pa + 1):
                epi_q.append(mk_bc(s2))

        def post_thunks(half):
            t0 = 8 * half
            th = []
            st = {}

            def ew_slice(c, hh):
                if half == 0:
                    return qall[:, 2048 * c + 512 * hh:2048 * c + 512 * hh + 512]
                return ewh2[:, 1024 * c + 512 * hh:1024 * c + 512 * hh + 512]

            def dump_dst():
                return qall[:, 0:1024] if half == 0 else qall[:, 1024:2048]

            def mk_proj(t, hh):
                def f():
                    r, psx = bg_bank()
                    wsem, wval = bg_rot[r]
                    wts = [(sem["WPT"], ld_wpT), (sem["V"], v_apply_last),
                           (sem["SET"], ld_small), (sem["V"], v_ones)]
                    if wval:
                        wts.append((wsem, wval))
                    tm = 0
                    for c in range(8):
                        tm = emit(
                            "t", "matmul", psx[:, 0:512],
                            attn[:, 2048 * c + 128 * t:2048 * c + 128 * t + 128],
                            wpT[:, 1024 * c + 512 * hh:1024 * c + 512 * hh + 512],
                            start=(c == 0), stop=False,
                            w=(wts if c == 0 else ()),
                            i=([("T", 1)] if c == 7 else ()))
                    tb = emit("t", "matmul", psx[:, 0:512],
                              onesr, bpb[:, 512 * hh:512 * hh + 512],
                              start=False, stop=True,
                              w=[(sem["T"], tm)], i=[("T", 1)])
                    vv = emit("v", "tensor_tensor",
                              stream_t[t][:, 512 * hh:512 * hh + 512],
                              psx[:, 0:512],
                              stream_t[t][:, 512 * hh:512 * hh + 512],
                              ALU.add,
                              w=[(sem["T"], tb),
                                 (sem["LSTR"], st["ld_stream"])],
                              i=[("V", 1)])
                    bg_rot[r] = (sem["V"], vv)
                    st[("res", t, hh)] = vv
                return f

            def mk_ln1_stats(t):
                def f():
                    v1 = emit("v", "reduce_sum", lnm16[:, t:t + 1],
                              stream_t[t], AX,
                              w=[(sem["V"], st[("res", t, 0)]),
                                 (sem["V"], st[("res", t, 1)])],
                              i=[("V", 1)])
                    wq_ = [(sem["V"], v1)]
                    if st.get(("dumpchain",), 0):
                        wq_.append((sem["V"], st[("dumpchain",)]))
                    vq = emit("v", "scalar_tensor_tensor", dump_dst(),
                              stream_t[t], 1.0, stream_t[t],
                              ALU.mult, ALU.mult,
                              w=wq_, i=[("V", 1)],
                              accum_out=lnq16[:, t:t + 1])
                    st[("stats", t)] = vq
                    st[("dumpchain",)] = vq
                return f

            def mk_ln_batch(which, c0, cnt=8):
                def f():
                    key = "stats" if which.startswith("ln1") else "stats2"
                    wts = [(sem["V"], st[(key, t)])
                           for t in range(c0, c0 + cnt)]
                    v2 = emit("v", "tensor_scalar", lnm16[:, c0:c0 + cnt],
                              lnm16[:, c0:c0 + cnt], 1.0 / D, None, ALU.mult,
                              w=wts, i=[("V", 1)])
                    v3 = emit("v", "tensor_scalar", lnq16[:, c0:c0 + cnt],
                              lnq16[:, c0:c0 + cnt], 1.0 / D, None, ALU.mult,
                              w=[(sem["V"], v2)], i=[("V", 1)])
                    v4 = emit("v", "tensor_tensor", lnv16[:, c0:c0 + cnt],
                              lnm16[:, c0:c0 + cnt], lnm16[:, c0:c0 + cnt],
                              ALU.mult, w=[(sem["V"], v3)], i=[("V", 1)])
                    v5 = emit("v", "tensor_tensor", lnv16[:, c0:c0 + cnt],
                              lnq16[:, c0:c0 + cnt], lnv16[:, c0:c0 + cnt],
                              ALU.subtract, w=[(sem["V"], v4)], i=[("V", 1)])
                    v5e = emit("v", "tensor_scalar", lnv16[:, c0:c0 + cnt],
                               lnv16[:, c0:c0 + cnt], 1e-5, None, ALU.add,
                               w=[(sem["V"], v5)], i=[("V", 1)])
                    sx = emit("s", "activation", lnr16[:, c0:c0 + cnt],
                              lnv16[:, c0:c0 + cnt], AF.Ln,
                              w=[(sem["V"], v5e)], i=[("S", 1)])
                    sy = emit("s", "activation", lnr16[:, c0:c0 + cnt],
                              lnr16[:, c0:c0 + cnt], AF.Exp, scale=-0.5,
                              w=[(sem["S"], sx)], i=[("S", 1)])
                    st[("rstd", which)] = sy
                return f

            def mk_ln1_apply(t):
                def f():
                    vv = emit("v", "tensor_scalar", stream_t[t], stream_t[t],
                              lnm16[:, t:t + 1], lnr16[:, t:t + 1],
                              ALU.subtract, ALU.mult,
                              w=[(sem["S"], st[("rstd", "ln1")])],
                              i=[("V", 1)])
                    st[("h1", t)] = vv
                return f

            def mk_tr(t, hh):
                def f():
                    r, psx = bg_bank()
                    wsem, wval = bg_rot[r]
                    wts = [(sem["V"], st[("h1", t)]), (sem["EYE"], ld_eye)]
                    if wval:
                        wts.append((wsem, wval))
                    tm = 0
                    for c4 in range(4):
                        c = 4 * hh + c4
                        tm = emit("t", "transpose",
                                  psx[:, 128 * c4:128 * c4 + 128],
                                  stream_t[t][:, 128 * c:128 * c + 128],
                                  eyeb,
                                  w=(wts if c4 == 0 else ()),
                                  i=([("T", 1)] if c4 == 3 else ()))
                    vv = emit("v", "tensor_copy",
                              ap(h1fm, 2048 * (4 * hh) + 128 * t,
                                 [[16384, 128], [2048, 4], [1, 128]]),
                              psx[:, 0:512],
                              w=[(sem["T"], tm)], i=[("V", 1)])
                    bg_rot[r] = (sem["V"], vv)
                    st[("h1fm", t, hh)] = vv
                return f

            def mk_router(t):
                def f():
                    r, psx = bg_bank()
                    wsem, wval = bg_rot[r]
                    wts = [(sem["V"], st[("h1fm", t, 0)]),
                           (sem["V"], st[("h1fm", t, 1)]),
                           (sem["SET"], ld_small)]
                    if wval:
                        wts.append((wsem, wval))
                    tm = 0
                    for c in range(8):
                        tm = emit("t", "matmul", psx[:, 0:8],
                                  h1fm[:, 2048 * c + 128 * t:2048 * c + 128 * t + 128],
                                  rwb[:, 8 * c:8 * c + 8],
                                  start=(c == 0), stop=(c == 7),
                                  w=(wts if c == 0 else ()),
                                  i=([("T", 1)] if c == 7 else ()))
                    vv = emit("v", "tensor_copy", logit[:, 8 * t:8 * t + 8],
                              psx[:, 0:8], w=[(sem["T"], tm)], i=[("V", 1)])
                    bg_rot[r] = (sem["V"], vv)
                    st[("logit", t)] = vv
                return f

            def gating():
                c0 = 8 * t0
                g0 = t0
                pr_ = ap(probs, c0, [[128, 128], [8, 8], [1, 8]])
                ga_ = ap(gates, c0, [[128, 128], [8, 8], [1, 8]])
                fl_p = ap(probs, c0, [[128, 128], [1, 64]])
                fl_g = ap(gates, c0, [[128, 128], [1, 64]])

                def b16(a):
                    return ap(a, g0, [[16, 128], [1, 8], [0, 8]])

                wts = [(sem["V"], st[("logit", t)]) for t in range(t0, t0 + 8)]
                sx = emit("s", "activation", fl_p,
                          ap(logit, c0, [[128, 128], [1, 64]]), AF.Exp,
                          w=wts, i=[("S", 1)])
                vv = emit("v", "reduce_sum", g_rd[:, g0:g0 + 8], pr_, AX,
                          w=[(sem["S"], sx)], i=[("V", 1)])
                vv = emit("v", "reciprocal", g_rd[:, g0:g0 + 8],
                          g_rd[:, g0:g0 + 8], w=[(sem["V"], vv)], i=[("V", 1)])
                vv = emit("v", "tensor_tensor", fl_p, fl_p, b16(g_rd),
                          ALU.mult, w=[(sem["V"], vv)], i=[("V", 1)])
                vv = emit("v", "reduce_max", g_v1[:, g0:g0 + 8], pr_, AX,
                          w=[(sem["V"], vv)], i=[("V", 1)])
                vv = emit("v", "tensor_tensor", fl_g, fl_p, b16(g_v1),
                          ALU.is_ge, w=[(sem["V"], vv)], i=[("V", 1)])
                vv = emit("v", "scalar_tensor_tensor", fl_g, fl_g, -100.0,
                          fl_p, ALU.mult, ALU.add,
                          w=[(sem["V"], vv)], i=[("V", 1)])
                vv = emit("v", "reduce_max", g_v2[:, g0:g0 + 8], ga_, AX,
                          w=[(sem["V"], vv)], i=[("V", 1)])
                vv = emit("v", "tensor_tensor", fl_g, fl_p, b16(g_v2),
                          ALU.is_ge, w=[(sem["V"], vv)], i=[("V", 1)])
                vv = emit("v", "tensor_tensor", g_rd[:, g0:g0 + 8],
                          g_v1[:, g0:g0 + 8], g_v2[:, g0:g0 + 8], ALU.add,
                          w=[(sem["V"], vv)], i=[("V", 1)])
                vv = emit("v", "tensor_scalar", g_rd[:, g0:g0 + 8],
                          g_rd[:, g0:g0 + 8], 1e-6, None, ALU.add,
                          w=[(sem["V"], vv)], i=[("V", 1)])
                vv = emit("v", "reciprocal", g_rd[:, g0:g0 + 8],
                          g_rd[:, g0:g0 + 8], w=[(sem["V"], vv)], i=[("V", 1)])
                vv = emit("v", "tensor_tensor", fl_g, fl_g, fl_p, ALU.mult,
                          w=[(sem["V"], vv)], i=[("V", 1)])
                vv = emit("v", "tensor_tensor", fl_g, fl_g, b16(g_rd),
                          ALU.mult, w=[(sem["V"], vv)], i=[("V", 1)])
                vv = emit("v", "tensor_tensor", g_t[:, g0:g0 + 8],
                          g_v1[:, g0:g0 + 8], g_v1[:, g0:g0 + 8], ALU.mult,
                          w=[(sem["V"], vv)], i=[("V", 1)])
                vv = emit("v", "tensor_tensor", g_v2[:, g0:g0 + 8],
                          g_v2[:, g0:g0 + 8], g_v2[:, g0:g0 + 8], ALU.mult,
                          w=[(sem["V"], vv)], i=[("V", 1)])
                vv = emit("v", "tensor_tensor", g_t[:, g0:g0 + 8],
                          g_t[:, g0:g0 + 8], g_v2[:, g0:g0 + 8], ALU.add,
                          w=[(sem["V"], vv)], i=[("V", 1)])
                v_g = emit("v", "tensor_tensor", g_t[:, g0:g0 + 8],
                           g_t[:, g0:g0 + 8], g_rd[:, g0:g0 + 8], ALU.mult,
                           w=[(sem["V"], vv)], i=[("V", 1)])
                st[("gates",)] = v_g

            def mk_ewload(e, hh, ci):
                def f():
                    wts = []
                    u = st.get(("ewuse", hh), 0)
                    if u:
                        wts.append((sem["T"], u))
                    if e == 0 and half == 0:
                        wts += [(sem["V"], st[("stats", t)])
                                for t in range(t0, t0 + 8)]
                    if e == 0 and half == 1:
                        wts.append((sem["T"], st[("projdone",)]))
                    if half == 0:
                        dst = ap(qall, 512 * hh + 2048 * 2 * ci,
                                 [[16384, 128], [2048, 2], [1, 512]])
                    else:
                        dst = ap(ewh2, 512 * hh + 1024 * 2 * ci,
                                 [[8192, 128], [1024, 2], [1, 512]])
                    st[("ewld", e, hh)] = emit(
                        "g", "dma_start", dst,
                        ap(d_ew, e * D * D + 512 * hh + 131072 * 2 * ci,
                           [[1024, 128], [131072, 2], [1, 512]]),
                        w=wts, i=[("EW%d" % hh, 16)])
                return f

            def mk_exp(e, t, hh):
                def f():
                    r, psx = bg_bank()
                    wsem, wval = bg_rot[r]
                    wts = [(sem["EW%d" % hh], st[("ewld", e, hh)]),
                           (sem["V"], st[("gates",)])]
                    if wval:
                        wts.append((wsem, wval))
                    tm = 0
                    for c in range(8):
                        tm = emit(
                            "t", "matmul", psx[:, 0:512],
                            h1fm[:, 2048 * c + 128 * t:2048 * c + 128 * t + 128],
                            ew_slice(c, hh),
                            start=(c == 0), stop=(c == 7),
                            w=(wts if c == 0 else ()),
                            i=([("T", 1)] if c == 7 else ()))
                    st[("ewuse", hh)] = tm
                    prev = st.get(("moe", t, hh), st[("h1", t)])
                    vv = emit("v", "scalar_tensor_tensor",
                              stream_t[t][:, 512 * hh:512 * hh + 512],
                              psx[:, 0:512],
                              gates[:, 8 * t + e:8 * t + e + 1],
                              stream_t[t][:, 512 * hh:512 * hh + 512],
                              ALU.mult, ALU.add,
                              w=[(sem["T"], tm), (sem["V"], prev)],
                              i=[("V", 1)])
                    bg_rot[r] = (sem["V"], vv)
                    st[("moe", t, hh)] = vv
                return f

            def mk_ln2_stats(t):
                def f():
                    vb_ = emit("v", "tensor_scalar", stream_t[t],
                               stream_t[t], g_t[:, t:t + 1], None, ALU.add,
                               w=[(sem["V"], st[("moe", t, 0)]),
                                  (sem["V"], st[("moe", t, 1)]),
                                  (sem["S"], st[("rstd", "ln1")])],
                               i=[("V", 1)])
                    v1 = emit("v", "reduce_sum", lnm16[:, t:t + 1],
                              stream_t[t], AX,
                              w=[(sem["V"], vb_)], i=[("V", 1)])
                    wq_ = [(sem["V"], v1),
                           (sem["V"], st[("dumpchain",)]),
                           (sem["EW1"], st[("ewld", NEXP - 1, 1)]),
                           (sem["T"], st[("ewuse", 0)]),
                           (sem["T"], st[("ewuse", 1)])]
                    vq = emit("v", "scalar_tensor_tensor", dump_dst(),
                              stream_t[t], 1.0, stream_t[t],
                              ALU.mult, ALU.mult,
                              w=wq_, i=[("V", 1)],
                              accum_out=lnq16[:, t:t + 1])
                    st[("stats2", t)] = vq
                    st[("dumpchain",)] = vq
                    if debug:
                        st[("r2d", t)] = emit(
                            "g", "dma_start",
                            dbg["res2"][:, 1024 * t:1024 * t + 1024],
                            stream_t[t], w=[(sem["V"], vq)], i=[("DBG", 16)])
                return f

            def mk_ln2_apply(t, rk):
                def f():
                    wts = [(sem["S"], st[("rstd", rk)])]
                    if debug:
                        wts.append((sem["DBG"], st[("r2d", t)]))
                    vv = emit("v", "tensor_scalar", stream_t[t], stream_t[t],
                              lnm16[:, t:t + 1], lnr16[:, t:t + 1],
                              ALU.subtract, ALU.mult,
                              w=wts,
                              i=[("V", 1)])
                    q = "g" if (half == 1 and t % 2 == 1) else "y"
                    emit(q, "dma_start",
                         ap(d_out, 1024 * 128 * t, [[1024, 128], [1, 1024]]),
                         stream_t[t],
                         w=[(sem["V"], vv)], i=[("LO", 16)])
                return f

            def projdone():
                st[("projdone",)] = ctr["T"]

            for t in range(t0, t0 + 8):
                th.append((1, mk_proj(t, 0)))
                th.append((1, mk_proj(t, 1)))
                th.append((2, mk_ln1_stats(t)))
            th.append((0, projdone))
            th.append((2, mk_ln_batch("ln1", t0)))
            for t in range(t0, t0 + 8):
                th.append((2, mk_ln1_apply(t)))
                th.append((1, mk_tr(t, 0)))
                th.append((1, mk_tr(t, 1)))
            for t in range(t0, t0 + 8):
                th.append((1, mk_router(t)))
            th.append((2, gating))
            for ci in range(4):
                th.append((0, mk_ewload(0, 0, ci)))
            for e in range(NEXP):
                for hh in range(2):
                    for t in range(t0, t0 + 8):
                        th.append((1, mk_exp(e, t, hh)))
                        if not (e == NEXP - 1 and hh == 1) and t - t0 >= 4:
                            ne, nh = (e, 1) if hh == 0 else (e + 1, 0)
                            th.append((0, mk_ewload(ne, nh, t - t0 - 4)))
            tail_th = []
            for g4 in range(2):
                b0 = t0 + 4 * g4
                for t in range(b0, b0 + 4):
                    th.append((2, mk_ln2_stats(t)))
                rk = "ln2" + ("a" if g4 == 0 else "b")
                th.append((2, mk_ln_batch(rk, b0, 4)))
                for t in range(b0, b0 + 4):
                    (tail_th if half == 0 else th).append(
                        (0, mk_ln2_apply(t, rk)))
            return th, st, tail_th

        seq = [(hp, p) for p in (0, 1) for hp in range(NHP)]
        kvq = iter(kv_groups(0, 0, 0))
        for g in kvq:
            g()
        st0 = None
        spb_user = {0: 0, 1: 0}
        for si, (hp, p) in enumerate(seq):
            kv_done = [False]
            if 1 <= si <= 4:
                load_wpT_chunk(si - 1)
            if si == 2:
                ld_stream1 = emit("y", "dma_start", str8,
                                  ap(d_xot, 0,
                                     [[1024, 128], [1024 * 128, 8], [1, 1024]]),
                                  w=[(sem["T"], guard[7])], i=[("LSTR", 16)])
            if p == 1 and hp == 0:
                th0, st0, tail0 = post_thunks(0)
                st0["ld_stream"] = ld_stream1
                work_q.extend(th0)
            nxt = seq[si + 1] if si + 1 < len(seq) else None
            kvq = iter(kv_groups(nxt[0], nxt[1], si + 1)) if nxt else iter(())
            buf = hp % 2
            kbase, vbase = 4096 * buf, 4160 * buf
            kr = kv_reload.get((hp, p))
            for slot in PASS_SLOTS[p]:
                npair = PAIRS[slot]
                for j in range(npair):
                    wts = []
                    if s_exp_prev:
                        wts.append((sem["S"], s_exp_prev))
                    else:
                        wts.append((sem["S"], qall_s[NHP - 1]))
                    tcn = (2 * j + 1) // 4
                    if p == 0 or tcn >= 4:
                        wts.append((sem["V"], kv_kevac[(hp, p)][tcn]))
                    if slot == PASS_SLOTS[p][0] and j == 0 and kr:
                        wts.append((sem[kr[0]], kr[1]))
                    t_sc = 0
                    for kk in range(2):
                        kgi = 2 * j + kk
                        for h in range(2):
                            r0, r1 = (0, 64) if h == 0 else (64, 128)
                            t_sc = emit(
                                "t", "matmul",
                                psS[:, 1024 * kk + 512 * h:1024 * kk + 512 * h + 512],
                                kb[r0:r1, kbase + 128 * kgi:kbase + 128 * kgi + 128],
                                qall[r0:r1, 2048 * hp + 512 * slot:2048 * hp + 512 * slot + 512],
                                start=True, stop=True,
                                w=(wts if (kk == 0 and h == 0) else ()),
                                i=([("T", 1)] if (kk == 1 and h == 1) else ()))
                    if len(av_queue) >= 2:
                        pop_av()
                    if not (slot == PASS_SLOTS[p][0] and j == 0):
                        for _ in range(2):
                            ng = next(kvq, None)
                            if ng is not None:
                                ng()
                            elif not kv_done[0]:
                                kv_done[0] = True
                                if si + 2 < len(seq):
                                    load_wq_usage(8 + si + 2,
                                                  kv_tlast[seq[si]])
                                    nhp, npp = seq[si + 2]
                                    for tc in ((0, 1) if npp == 0 else (4, 5)):
                                        if (npp, nhp, tc) not in xk_load:
                                            load_xk(npp, nhp, tc)
                    v_cool[0] = max(0, v_cool[0] - 1)
                    heavy_done = 0
                    pops = 0
                    if slot == PASS_SLOTS[p][1] and j >= npair - 2:
                        heavy_done = 1
                    if epi_q:
                        epi_q.pop(0)()
                        epi_pops[0] += 1
                    while (work_q and pops < 3
                           and epi_pops[0] >= P0_EPI):
                        tag = work_q[0][0]
                        if tag == 1 and heavy_done:
                            break
                        if tag == 2 and (heavy_done or v_cool[0]):
                            break
                        fn = work_q.pop(0)[1]
                        fn()
                        pops += 1
                        if tag:
                            heavy_done = 1
                        if tag == 2:
                            v_cool[0] = 3
                    par = gpair[0] % 2
                    gpair[0] += 1
                    wexp = [(sem["T"], t_sc)]
                    s_exp_prev = emit("s", "activation", t_exp,
                                      psS[:, 0:2048], AF.Exp,
                                      w=wexp, i=[("S", 1)], scale=0.125)
                    wln = [(sem["S"], s_exp_prev)]
                    if spb_user[0]:
                        wln.append((sem["V"], spb_user[0]))
                    s_ln_prev = emit("s", "activation", spb, t_exp, AF.Ln,
                                     w=wln, i=[("S", 1)], bias=1.0)
                    v_sq_raw = emit("v", "tensor_tensor",
                                    pbb[:, 2048 * par:2048 * par + 2048],
                                    spb, spb, ALU.mult,
                                    w=[(sem["S"], s_ln_prev),
                                       (sem["T"], t_av_prev)], i=[("V", 1)])
                    spb_user[0] = v_sq_raw
                    v_sq_prev = v_sq_raw
                    if j >= npair - 4:
                        vw = [(sem["V"], v_sq_raw)]
                        if not mask_waited[0]:
                            vw.append((sem["LSC"], ld_mask))
                            mask_waited[0] = True
                        m0 = 2 * (j - (npair - 4))
                        v_sq_prev = emit(
                            "v", "tensor_tensor",
                            ap(pbb, 2048 * par,
                               [[4096, 128], [1024, 2], [512, 2], [1, 512]]),
                            ap(pbb, 2048 * par,
                               [[4096, 128], [1024, 2], [512, 2], [1, 512]]),
                            ap(mkb, 512 * m0,
                               [[4096, 128], [512, 2], [0, 2], [1, 512]]),
                            ALU.mult, w=vw, i=[("V", 1)])
                    av_queue.append((slot, j, v_sq_prev, par, hp, p,
                                     j == npair - 1))
            for g in kvq:
                g()
            if not kv_done[0] and si + 2 < len(seq):
                load_wq_usage(8 + si + 2, kv_tlast[seq[si + 1]])
                nhp, npp = seq[si + 2]
                for tc in ((0, 1) if npp == 0 else (4, 5)):
                    if (npp, nhp, tc) not in xk_load:
                        load_xk(npp, nhp, tc)
        while av_queue:
            pop_av()
        while epi_q:
            epi_q.pop(0)()

        if debug:
            emit("g", "dma_start", dbg["attn"], attn,
                 w=[(sem["V"], ctr["V"]), (sem["T"], ctr["T"])],
                 i=[("DBG", 16)])
            emit("g", "dma_start", dbg["kb"], kb, i=[("DBG", 16)])
            emit("g", "dma_start", dbg["vb"], vb, i=[("DBG", 16)])
            emit("g", "dma_start", dbg["qall"], qall, i=[("DBG", 16)])

        th1, st1, tail1 = post_thunks(1)
        w_s2 = [(sem["T"], ctr["T"]), (sem["V"], ctr["V"]),
                (sem["S"], ctr["S"])]
        emit("y", "dma_start", str4a,
             ap(d_xot, 8 * 128 * 1024,
                [[1024, 128], [1024 * 128, 4], [1, 1024]]),
             w=w_s2, i=[("LSTR", 16)])
        emit("y", "dma_start", str4b,
             ap(d_xot, 12 * 128 * 1024,
                [[1024, 128], [1024 * 128, 4], [1, 1024]]),
             w=w_s2, i=[("LSTR", 16)])
        st1["ld_stream"] = ctr["LSTR"]
        q1 = list(th1)
        while work_q or tail0 or q1:
            if work_q:
                work_q.pop(0)[1]()
            elif tail0:
                tail0.pop(0)[1]()
            if q1:
                q1.pop(0)[1]()
        if debug:
            emit("g", "dma_start", dbg["h1fm"], h1fm,
                 w=[(sem["V"], ctr["V"])], i=[("DBG", 16)])
            emit("g", "dma_start", dbg["gates"], gates, i=[("DBG", 16)])
            emit("g", "dma_start", dbg["bias16"], g_t, i=[("DBG", 16)])
        emit("y", "wait_ge", sem["LO"], ctr["LO"])

        block = En(nc.Block())

        def runner(key):
            def run(eng):
                for waits, method, args, kw, incs in recs[key]:
                    for s, val in waits:
                        if val:
                            eng.wait_ge(s, val)
                    if method == "wait_ge":
                        eng.wait_ge(*args)
                        continue
                    instr = getattr(eng, method)(*args, **kw)
                    for nm, n in incs:
                        instr.then_inc(sem[nm], n)
            return run

        block.tensor(runner("t"))
        block.scalar(runner("s"))
        block.vector(runner("v"))
        block.gpsimd(runner("g"))
        block.sync(runner("y"))

    return nc



def _role_masks(role):
    q = np.arange(QS)[None, :]
    k = np.arange(128)[:, None]
    tiles = []
    for j in range(8):
        if role == 0:
            m = (np.ones((128, QS), np.float32) if j < 4
                 else (q >= 128 * (j - 4) + k).astype(np.float32))
        else:
            m = ((q >= 128 * j + k).astype(np.float32) if j < 4
                 else np.zeros((128, QS), np.float32))
        tiles.append(m)
    return np.ascontiguousarray(
        np.concatenate(tiles, axis=1).astype(ml_dtypes.bfloat16))


_NC_CACHE = {}


def kernel(L0_attn_x, L0_attn_w_qkv, L0_attn_w_proj, L0_attn_b_proj,
           L0_ffn_router_w, L0_ffn_expert_weights):
    x = np.asarray(L0_attn_x, np.float32)
    w_qkv = np.asarray(L0_attn_w_qkv, np.float32)
    w_proj = np.asarray(L0_attn_w_proj, np.float32)
    b_proj = np.asarray(L0_attn_b_proj, np.float32)
    router_w = np.asarray(L0_ffn_router_w, np.float32)
    ew = np.ascontiguousarray(np.asarray(L0_ffn_expert_weights, np.float32))

    wq8 = np.empty((NHP, D, 384), np.float32)
    for hp in range(NHP):
        rows = np.r_[128 * hp:128 * hp + 128,
                     1024 + 128 * hp:1024 + 128 * hp + 128,
                     2048 + 128 * hp:2048 + 128 * hp + 128]
        wq8[hp] = w_qkv[rows, :].T
    wpT = np.ascontiguousarray(w_proj.T)
    rwT = np.ascontiguousarray(router_w.T)
    bp = np.ascontiguousarray(b_proj[None, :])
    eye = np.eye(128, dtype=np.float32)
    masks = {r: _role_masks(r) for r in (0, 1)}

    in_maps = []
    blocks = {}
    for core in range(8):
        b, role = core // 2, core % 2
        gs = [e - 512 for e in ROLE_EXT[role]]
        blocks[core] = gs
        own = np.concatenate([np.arange(g, g + 512) for g in gs])
        xb = x[b]
        in_maps.append({
            "xkf": np.ascontiguousarray(xb.T),
            "xqf": np.ascontiguousarray(xb[own].T),
            "xot": np.ascontiguousarray(xb[own]),
            "wq8": wq8, "wpT": wpT, "bp": bp, "rwT": rwT, "ew": ew,
            "mk": masks[role], "eyein": eye,
        })

    if "nc" not in _NC_CACHE:
        _NC_CACHE["nc"] = build_program()
    nc = _NC_CACHE["nc"]
    res = run_bass_kernel_spmd(nc, in_maps, core_ids=list(range(8)))
    out = np.empty((B, T, D), np.float32)
    for core in range(8):
        co = np.asarray(res.results[core]["out"], np.float32)
        b = core // 2
        for s, g in enumerate(blocks[core]):
            out[b, g:g + 512] = co[512 * s:512 * (s + 1)]
    return out
